# version 02
# baseline (speedup 1.0000x reference)
import contextlib
import numpy as np

import concourse.bass as bass
import concourse.mybir as mybir
from concourse.bass_utils import run_bass_kernel_spmd

F32 = mybir.dt.float32
BF16 = mybir.dt.bfloat16
FP8 = mybir.dt.float8e4
AX = mybir.AxisListType
ALU = mybir.AluOpType
ACT = mybir.ActivationFunctionType

NCORES = 8
B = 2
S = 4096
D = 2048
TOK = 1024
TT = TOK // 128
H = 16
DH = 128
LB = 256
NLB = TOK // LB
I_FFN = 8192
QL = 1536
KVL = 512
RD = 64
MAGIC = 12582912.0
ENGS = ("tensor", "vector", "scalar", "gpsimd", "sync")


class Sem:
    def __init__(self, handle):
        self.h = handle
        self.n = 0

    def add(self, k=1):
        self.n += k
        return self.n


class Prog:
    def __init__(self, name="prog"):
        self.nc = bass.Bass("TRN2", target_bir_lowering=False,
                            detect_race_conditions=False)
        self.es = contextlib.ExitStack()
        self.name = name
        self._i = 0

    def sem(self):
        self._i += 1
        return Sem(self.es.enter_context(self.nc.semaphore(f"s{self._i}")))

    def sbuf(self, name, shape, dtype):
        return self.es.enter_context(self.nc.sbuf_tensor(name, shape, dtype))

    def psum(self, name, shape, dtype=F32):
        return self.es.enter_context(self.nc.psum_tensor(name, shape, dtype))

    def din(self, name, shape, dtype=F32):
        return self.nc.dram_tensor(name, list(shape), dtype,
                                   kind="ExternalInput").ap()

    def dout(self, name, shape, dtype=F32):
        return self.nc.dram_tensor(name, list(shape), dtype,
                                   kind="ExternalOutput").ap()

    def dint(self, name, shape, dtype=F32):
        return self.nc.dram_tensor(name, list(shape), dtype,
                                   kind="Internal").ap()

    def block(self):
        return self.es.enter_context(self.nc.Block())


def ts(i, n=128):
    return slice(i * n, (i + 1) * n)



def emit_ln(p, block, *, x_ap, out_ap, stat, scr6, eps=1e-5,
            wait_v=None, done=None):
    st = p.sem()

    @block.vector
    def _(v):
        for w in (wait_v or []):
            v.wait_ge(w[0].h, w[1])
        width = x_ap.shape[-1]
        nch = (width + 511) // 512
        for j in range(nch):
            v.bn_stats(scr6[:, j * 6:(j + 1) * 6],
                       x_ap[:, j * 512:min((j + 1) * 512, width)])
            v.drain()
        v.bn_aggr(stat[:, 0:2], scr6[:, 0:6 * nch])
        v.drain()
        v.tensor_scalar(stat[:, 1:2], stat[:, 1:2], eps, None,
                        ALU.add).then_inc(st.h, 1)
    st.add(1)

    @block.scalar
    def _(s):
        s.wait_ge(st.h, st.n)
        s.activation(stat[:, 2:3], stat[:, 1:2], ACT.Sqrt).then_inc(st.h, 1)
    st.add(1)

    @block.vector
    def _(v):
        v.wait_ge(st.h, st.n)
        v.reciprocal(stat[:, 3:4], stat[:, 2:3])
        v.drain()
        v.scalar_tensor_tensor(stat[:, 4:5], stat[:, 3:4], 1.0,
                               stat[:, 3:4], ALU.mult, ALU.mult)
        v.drain()
        v.scalar_tensor_tensor(stat[:, 4:5], stat[:, 4:5], -0.5,
                               stat[:, 1:2], ALU.mult, ALU.mult)
        v.drain()
        v.tensor_scalar(stat[:, 4:5], stat[:, 4:5], 1.5, None, ALU.add)
        v.drain()
        v.tensor_tensor(stat[:, 3:4], stat[:, 3:4], stat[:, 4:5], ALU.mult)
        v.drain()
        v.scalar_tensor_tensor(stat[:, 5:6], stat[:, 0:1], -1.0,
                               stat[:, 3:4], ALU.mult, ALU.mult)
        v.drain()
        ins = v.tensor_scalar(out_ap, x_ap, stat[:, 3:4], stat[:, 5:6],
                              ALU.mult, ALU.add)
        if done is not None:
            ins.then_inc(done.h, 1)
    if done is not None:
        done.add(1)


def emit_transpose(p, block, *, tiles, ident, pt, evac="vector",
                   wait_t=None, sems=None):
    if sems is None:
        sems = (p.sem(), p.sem())
    td, ed = sems
    n = len(tiles)
    tbase = td.n
    assert tbase == ed.n

    @block.tensor
    def _(t):
        for w in (wait_t or []):
            t.wait_ge(w[0].h, w[1])
        for i, (src, _dst) in enumerate(tiles):
            gi = tbase + i
            if gi >= 2:
                t.wait_ge(ed.h, gi - 1)
            fp, ff = src.shape[0], src.shape[-1]
            t.transpose(pt[gi % 2][0:ff, 0:fp], src,
                        ident[:]).then_inc(td.h, 1)
    td.add(n)

    @getattr(block, evac)
    def _(e):
        for i, (src, dst) in enumerate(tiles):
            gi = tbase + i
            e.wait_ge(td.h, gi + 1)
            fp, ff = src.shape[0], src.shape[-1]
            if evac == "scalar":
                ins = e.copy(dst, pt[gi % 2][0:ff, 0:fp])
            else:
                ins = e.tensor_copy(dst, pt[gi % 2][0:ff, 0:fp])
            ins.then_inc(ed.h, 1)
    ed.add(n)
    return ed


class Panels:

    def __init__(self, p, name, n_k, nb):
        self.p = p
        self.n_k = n_k
        self.nb = nb
        self.sb = p.sbuf(name, [128, 2 * n_k * nb], BF16)
        self.loaded = p.sem()
        self.rel = []
        self.gen = 0

    def buf(self, gen, k):
        o = (gen % 2) * self.n_k * self.nb + k * self.nb
        return self.sb[:, o:o + self.nb]

    def load_gen(self, block, srcs, cast, release):
        gen = self.gen
        self.rel.append(release)
        assert len(self.rel) == gen + 1
        me = self

        def body(e):
            if gen >= 2:
                rs, rv = me.rel[gen - 2]
                e.wait_ge(rs.h, rv)
            for k, ap in enumerate(srcs):
                e.dma_start(out=me.buf(gen, k), in_=ap).then_inc(me.loaded.h, 16)

        if cast:
            @block.gpsimd
            def _(g):
                body(g)
        else:
            @block.sync
            def _(s):
                body(s)
        self.loaded.add(16 * self.n_k)
        self.gen += 1
        return self.loaded.n


def gemm_tm(p, block, *, lhsT_tile, k_chunks, n_chunks, nb, panels,
            panel_srcs, cast, evac_engine, evac_emit, m_tiles=TT,
            wait_t=None, psA=None, psB=None, tail=None):
    mmdone = p.sem()
    psfree = p.sem()
    base = mmdone.n
    gen0 = panels.gen
    gw = [panels.load_gen(block, panel_srcs(n), cast,
                          (mmdone, base + (n + 1) * m_tiles))
          for n in range(n_chunks)]

    @block.tensor
    def _(t):
        for w in (wait_t or []):
            t.wait_ge(w[0].h, w[1])
        idx = 0
        for n in range(n_chunks):
            t.wait_ge(panels.loaded.h, gw[n])
            for m in range(m_tiles):
                ps = psA if idx % 2 == 0 else psB
                if idx >= 2:
                    t.wait_ge(psfree.h, idx - 1)
                mi = None
                for k in range(k_chunks):
                    mi = t.matmul(ps[:, 0:nb], lhsT_tile(k, m),
                                  panels.buf(gen0 + n, k),
                                  start=(k == 0), stop=(k == k_chunks - 1))
                mi.then_inc(mmdone.h, 1)
                idx += 1
    mmdone.add(n_chunks * m_tiles)

    @getattr(block, evac_engine)
    def _(e):
        idx = 0
        for n in range(n_chunks):
            for m in range(m_tiles):
                e.wait_ge(mmdone.h, base + idx + 1)
                ps = psA if idx % 2 == 0 else psB
                ins = evac_emit(e, m, n, ps[:, 0:nb], idx)
                ins.then_inc(psfree.h, 1)
                idx += 1
        if tail is not None:
            tail(e)
    psfree.add(n_chunks * m_tiles)
    return psfree



def build_p1():
    p = Prog("p1")
    x_h = p.din("x", [TOK, D])
    wqkv_h = p.din("wqkvT", [D, 3 * D])
    wg_h = p.din("wgT", [D, D])
    slope_h = p.din("slope", [1, H])
    iota_h = p.din("iota6", [128, 6])
    ident_h = p.din("ident", [128, 128], BF16)
    qkv_o = p.dout("qkv", [TOK, 3 * D], BF16)
    L_o = p.dout("L", [128, H * DH])
    gate_o = p.dout("gate", [TOK, D], BF16)

    xt = p.sbuf("xt", [128, 2 * D], F32)
    xln = p.sbuf("xln", [128, 2 * D], BF16)
    xlnT = p.sbuf("xlnT", [128, (D // 128) * TOK], BF16)
    stat = p.sbuf("stat", [128, 8], F32)
    scr6 = p.sbuf("scr6", [128, 24], F32)
    ident = p.sbuf("identS", [128, 128], BF16)
    s_bc = p.sbuf("s_bc", [128, H], F32)
    iota6 = p.sbuf("iota6s", [128, 6], F32)
    dec = p.sbuf("dec", [128, H * 6], F32)
    bd = p.sbuf("bd", [128, H], F32)
    qkv = p.sbuf("qkvS", [128, TT * 3 * D], BF16)
    Lst = p.sbuf("Lst", [128, H * DH], F32)
    ktmp = p.sbuf("ktmp", [128, 2 * DH], BF16)
    gstg = p.sbuf("gstg", [128, 2 * 512], BF16)
    panels = Panels(p, "wpan", D // 128, 512)

    psA = p.psum("psA", [128, 512], F32)
    psB = p.psum("psB", [128, 512], F32)
    ptT = (p.psum("ptTa", [128, 128], BF16), p.psum("ptTb", [128, 128], BF16))
    psU = p.psum("psU", [128, DH], F32)

    block = p.block()

    ld = p.sem()
    tsem = (p.sem(), p.sem())
    trd = tsem[1]
    decd = p.sem()
    xload = p.sem()
    lnd = p.sem()
    spld = p.sem()
    gdma = p.sem()

    @block.sync
    def _(sy):
        sy.dma_start(out=ident[:], in_=ident_h).then_inc(ld.h, 16)
        sy.dma_start(out=iota6[:], in_=iota_h).then_inc(ld.h, 16)
        sy.dma_start(out=s_bc[:],
                     in_=slope_h.broadcast_to([128, H])).then_inc(ld.h, 16)
    ld.add(48)

    @block.vector
    def _(v):
        v.wait_ge(ld.h, ld.n)
        v.tensor_tensor(
            dec[:].rearrange("p (h j) -> p h j", j=6),
            iota6[:].unsqueeze(1).broadcast_to([128, H, 6]),
            s_bc[:].unsqueeze(2).broadcast_to([128, H, 6]),
            ALU.mult)
        v.drain()
        v.tensor_scalar(bd[:], s_bc[:], -float(LB), None,
                        ALU.mult).then_inc(decd.h, 1)
    decd.add(1)

    @block.scalar
    def _(s):
        s.wait_ge(decd.h, decd.n)
        s.activation(dec[:], dec[:], ACT.Exp)
        s.drain()
        s.activation(bd[:], bd[:], ACT.Exp).then_inc(decd.h, 1)
    decd.add(1)

    for m in range(TT):
        half = m % 2

        @block.sync
        def _(sy, m=m, half=half):
            if m >= 2:
                sy.wait_ge(trd.h, 16 * (m - 1))
            sy.dma_start(out=xt[:, half * D:(half + 1) * D],
                         in_=x_h[ts(m), :]).then_inc(xload.h, 16)
        xload.add(16)

        emit_ln(p, block, x_ap=xt[:, half * D:(half + 1) * D],
                out_ap=xln[:, half * D:(half + 1) * D],
                stat=stat, scr6=scr6,
                wait_v=[(xload, xload.n)], done=lnd)

        emit_transpose(
            p, block,
            tiles=[(xln[:, half * D + c * 128: half * D + (c + 1) * 128],
                    xlnT[:, c * TOK + m * 128: c * TOK + (m + 1) * 128])
                   for c in range(D // 128)],
            ident=ident, pt=ptT, evac="vector",
            wait_t=[(ld, ld.n), (lnd, lnd.n)], sems=tsem)

    def silu_evac(e, m, n, ps, idx):
        return e.activation(
            qkv[:, m * 3 * D + n * 512: m * 3 * D + (n + 1) * 512],
            ps, ACT.Silu)

    qk_ev = gemm_tm(p, block,
                    lhsT_tile=lambda k, m: xlnT[:, k * TOK + m * 128:
                                                k * TOK + (m + 1) * 128],
                    k_chunks=D // 128, n_chunks=3 * D // 512, nb=512,
                    panels=panels,
                    panel_srcs=lambda n: [wqkv_h[ts(k), n * 512:(n + 1) * 512]
                                          for k in range(D // 128)],
                    cast=True, evac_engine="scalar", evac_emit=silu_evac,
                    wait_t=[(trd, trd.n)], psA=psA, psB=psB)

    @block.sync
    def _(sy):
        sy.wait_ge(qk_ev.h, qk_ev.n)
        for m in range(TT):
            sy.dma_start(out=qkv_o[ts(m), :],
                         in_=qkv[:, m * 3 * D:(m + 1) * 3 * D]
                         ).then_inc(spld.h, 16)
    spld.add(16 * TT)

    vinit = p.sem()

    @block.vector
    def _(v):
        v.memset(Lst[:], 0.0).then_inc(vinit.h, 1)
    vinit.add(1)

    ustep = p.sem()
    for blk in range(NLB):
        for h in range(H):
            @block.vector
            def _(v, blk=blk, h=h):
                v.wait_ge(ustep.h, ustep.n)
                if blk == 0 and h == 0:
                    v.wait_ge(qk_ev.h, qk_ev.n)
                    v.wait_ge(decd.h, decd.n)
                    v.wait_ge(vinit.h, vinit.n)
                ins = None
                for half in range(2):
                    m = 2 * blk + half
                    kap = qkv[:, m * 3 * D + D + h * DH:
                              m * 3 * D + D + (h + 1) * DH]
                    ins = v.tensor_scalar(
                        ktmp[:, half * DH:(half + 1) * DH], kap,
                        dec[:, h * 6 + 2 + half:h * 6 + 3 + half],
                        None, ALU.mult)
                ins.then_inc(ustep.h, 1)
            ustep.add(1)

            @block.tensor
            def _(t, blk=blk, h=h):
                t.wait_ge(ustep.h, ustep.n)
                mi = None
                for half in range(2):
                    m = 2 * blk + half
                    vap = qkv[:, m * 3 * D + 2 * D + h * DH:
                              m * 3 * D + 2 * D + (h + 1) * DH]
                    mi = t.matmul(psU[:], ktmp[:, half * DH:(half + 1) * DH],
                                  vap, start=(half == 0), stop=(half == 1))
                mi.then_inc(ustep.h, 1)
            ustep.add(1)

            @block.vector
            def _(v, blk=blk, h=h):
                v.wait_ge(ustep.h, ustep.n)
                v.scalar_tensor_tensor(Lst[:, h * DH:(h + 1) * DH],
                                       Lst[:, h * DH:(h + 1) * DH],
                                       bd[:, h:h + 1], psU[:],
                                       ALU.mult, ALU.add).then_inc(ustep.h, 1)
            ustep.add(1)

    @block.sync
    def _(sy):
        sy.wait_ge(ustep.h, ustep.n)
        sy.dma_start(out=L_o, in_=Lst[:]).then_inc(spld.h, 16)
    spld.add(16)

    def sig_evac(e, m, n, ps, idx):
        if idx >= 2:
            e.wait_ge(gdma.h, 16 * (idx - 1))
        return e.activation(gstg[:, (idx % 2) * 512:(idx % 2 + 1) * 512],
                            ps, ACT.Sigmoid)

    g_ev = gemm_tm(p, block,
                   lhsT_tile=lambda k, m: xlnT[:, k * TOK + m * 128:
                                               k * TOK + (m + 1) * 128],
                   k_chunks=D // 128, n_chunks=D // 512, nb=512,
                   panels=panels,
                   panel_srcs=lambda n: [wg_h[ts(k), n * 512:(n + 1) * 512]
                                         for k in range(D // 128)],
                   cast=True, evac_engine="scalar", evac_emit=sig_evac,
                   wait_t=[(qk_ev, qk_ev.n)], psA=psA, psB=psB)

    @block.sync
    def _(sy):
        idx = 0
        for n in range(D // 512):
            for m in range(TT):
                sy.wait_ge(g_ev.h, g_ev.n - (D // 512) * TT + idx + 1)
                sy.dma_start(out=gate_o[ts(m), n * 512:(n + 1) * 512],
                             in_=gstg[:, (idx % 2) * 512:(idx % 2 + 1) * 512]
                             ).then_inc(gdma.h, 16)
                idx += 1
    gdma.add(16 * (D // 512) * TT)

    @block.gpsimd
    def _(g):
        g.wait_ge(spld.h, spld.n)
        g.wait_ge(gdma.h, gdma.n)

    return p



def _shard_tokens(x):
    flat = np.ascontiguousarray(np.asarray(x).reshape(B * S, -1))
    return [flat[c * TOK:(c + 1) * TOK] for c in range(NCORES)]


def _ident_bf16():
    import ml_dtypes
    return np.eye(128, dtype=ml_dtypes.bfloat16)


def _iota6():
    m = np.arange(128, dtype=np.float32)
    return np.ascontiguousarray(np.stack(
        [-(m + 1), -(m + 129), -(255 - m), -(127 - m), (m + 1), (m + 129)],
        axis=1))


def run_p1(inputs):
    x = np.asarray(inputs["L0_attn_hidden_states"], dtype=np.float32)
    wqkvT = np.ascontiguousarray(
        np.asarray(inputs["L0_attn_w_qkv"], dtype=np.float32).T)
    wgT = np.ascontiguousarray(
        np.asarray(inputs["L0_attn_w_gate"], dtype=np.float32).T)
    slope = np.ascontiguousarray(np.asarray(
        inputs["L0_attn_slope_rate"], dtype=np.float32).reshape(1, H))
    shards = _shard_tokens(x)
    p = build_p1()
    in_maps = [{"x": shards[c], "wqkvT": wqkvT, "wgT": wgT, "slope": slope,
                "iota6": _iota6(), "ident": _ident_bf16()}
               for c in range(NCORES)]
    res = run_bass_kernel_spmd(p.nc, in_maps, core_ids=list(range(NCORES)))
    return res.results



def build_p2():
    p = Prog("p2")
    x_h = p.din("x", [TOK, D])
    qkv_h = p.din("qkv", [TOK, 3 * D], BF16)
    Lall_h = p.din("Lall", [NCORES * 128, H * DH])
    gate_h = p.din("gate", [TOK, D], BF16)
    wout_h = p.din("woutT", [D, D])
    normw_h = p.din("normw", [1, D])
    slope_h = p.din("slope", [1, H])
    iota_h = p.din("iota6", [128, 6])
    ident_h = p.din("ident", [128, 128], BF16)
    mask_h = p.din("mask01", [128, 128], BF16)
    jprev_h = p.din("jprev", [1, 4])
    x1_o = p.dout("x1", [TOK, D])
    attn_dbg = p.dout("attn_dbg", [TOK, D], BF16)

    Lprev_h = p.din("Lprev", [3 * 128, H * DH])
    dist_h = p.din("dist", [1, 3])

    x_sb = p.sbuf("x_sb", [128, TT * D], F32)
    gate_sb = p.sbuf("gate_sb", [128, TT * D], BF16)
    attn_sb = p.sbuf("attn_sb", [128, 2 * D], BF16)
    gatedT = p.sbuf("gatedT", [128, (D // 128) * TOK], BF16)
    qblk = p.sbuf("qblk", [128, 2 * 3 * D], BF16)
    Ssb = p.sbuf("Ssb", [128, H * DH], F32)
    Sbf = p.sbuf("Sbf", [128, DH], BF16)
    qt = p.sbuf("qt", [128, 2 * DH], BF16)
    kit = p.sbuf("kit", [128, 2 * DH], BF16)
    ktmp = p.sbuf("ktmp", [128, 2 * DH], BF16)
    qfm = p.sbuf("qfm", [128, 2 * DH], BF16)
    kfm = p.sbuf("kfm", [128, 2 * DH], BF16)
    Pk0 = p.sbuf("Pk0", [128, 2 * DH], BF16)
    Pk1 = p.sbuf("Pk1", [128, 2 * DH], BF16)
    stat = p.sbuf("stat", [128, 8], F32)
    sqs = p.sbuf("sqs", [128, D], BF16)
    ident = p.sbuf("identS", [128, 128], BF16)
    mask01 = p.sbuf("mask01S", [128, 128], BF16)
    normw = p.sbuf("normwS", [128, D], BF16)
    s_bc = p.sbuf("s_bc", [128, H], F32)
    iota6 = p.sbuf("iota6s", [128, 6], F32)
    dec = p.sbuf("dec", [128, H * 6], F32)
    bd = p.sbuf("bd", [128, H], F32)
    dist = p.sbuf("distS", [128, 3], F32)
    coef = p.sbuf("coefS", [128, 3 * H], F32)
    panels = Panels(p, "wpan", D // 128, 256)

    psSa = p.psum("psSa", [128, 256], F32)
    psSb = p.psum("psSb", [128, 256], F32)
    psX = p.psum("psX", [128, DH], F32)
    ptT = (p.psum("ptTa", [128, 128], BF16), p.psum("ptTb", [128, 128], BF16))
    psA = p.psum("psA", [128, 512], F32)
    psB = p.psum("psB", [128, 512], F32)

    block = p.block()

    ld = p.sem()
    tsem = (p.sem(), p.sem())
    decd = p.sem()
    ust = p.sem()
    blkld = p.sem()
    adbg = p.sem()

    nwld = p.sem()

    @block.gpsimd
    def _(g):
        g.dma_start(out=normw[:],
                    in_=normw_h.broadcast_to([128, D])).then_inc(nwld.h, 16)
    nwld.add(16)

    nwld = p.sem()

    @block.gpsimd
    def _(g):
        g.dma_start(out=normw[:],
                    in_=normw_h.broadcast_to([128, D])).then_inc(nwld.h, 16)
    nwld.add(16)

    @block.sync
    def _(sy):
        sy.dma_start(out=ident[:], in_=ident_h).then_inc(ld.h, 16)
        sy.dma_start(out=mask01[:], in_=mask_h).then_inc(ld.h, 16)
        sy.dma_start(out=iota6[:], in_=iota_h).then_inc(ld.h, 16)
        sy.dma_start(out=s_bc[:],
                     in_=slope_h.broadcast_to([128, H])).then_inc(ld.h, 16)
        sy.dma_start(out=dist[:],
                     in_=dist_h.broadcast_to([128, 3])).then_inc(ld.h, 16)
        for m in range(TT):
            sy.dma_start(out=x_sb[:, m * D:(m + 1) * D],
                         in_=x_h[ts(m), :]).then_inc(ld.h, 16)
            sy.dma_start(out=gate_sb[:, m * D:(m + 1) * D],
                         in_=gate_h[ts(m), :]).then_inc(ld.h, 16)
    ld.add(16 * (5 + 2 * TT))

    @block.vector
    def _(v):
        v.wait_ge(ld.h, ld.n)
        v.wait_ge(nwld.h, nwld.n)
        v.tensor_tensor(
            dec[:].rearrange("p (h j) -> p h j", j=6),
            iota6[:].unsqueeze(1).broadcast_to([128, H, 6]),
            s_bc[:].unsqueeze(2).broadcast_to([128, H, 6]),
            ALU.mult)
        v.drain()
        v.tensor_scalar(bd[:], s_bc[:], -float(LB), None, ALU.mult)
        v.drain()
        v.tensor_tensor(
            coef[:].rearrange("p (j h) -> p j h", h=H),
            dist[:].unsqueeze(2).broadcast_to([128, 3, H]),
            s_bc[:].unsqueeze(1).broadcast_to([128, 3, H]),
            ALU.mult)
        v.drain()
        v.tensor_scalar(coef[:], coef[:], -1024.0, None,
                        ALU.mult).then_inc(decd.h, 1)
    decd.add(1)

    @block.scalar
    def _(s):
        s.wait_ge(decd.h, decd.n)
        s.activation(dec[:], dec[:], ACT.Exp)
        s.drain()
        s.activation(bd[:], bd[:], ACT.Exp)
        s.drain()
        s.activation(coef[:], coef[:], ACT.Exp).then_inc(decd.h, 1)
    decd.add(1)

    Lld = p.sem()
    sinit = p.sem()
    Lpr = qblk[:, 0:2 * H * DH].bitcast(F32)

    @block.vector
    def _(v):
        v.wait_ge(decd.h, decd.n)
        v.memset(Ssb[:], 0.0).then_inc(sinit.h, 1)
    sinit.add(1)

    for j in range(3):
        @block.sync
        def _(sy, j=j):
            sy.wait_ge(sinit.h, sinit.n)
            sy.dma_start(out=Lpr, in_=Lprev_h[ts(j), :]).then_inc(Lld.h, 16)
        Lld.add(16)

        @block.vector
        def _(v, j=j):
            v.wait_ge(Lld.h, Lld.n)
            ins = None
            for h in range(H):
                ins = v.scalar_tensor_tensor(
                    Ssb[:, h * DH:(h + 1) * DH],
                    Lpr[:, h * DH:(h + 1) * DH],
                    coef[:, j * H + h:j * H + h + 1],
                    Ssb[:, h * DH:(h + 1) * DH],
                    ALU.mult, ALU.add)
            ins.then_inc(sinit.h, 1)
        sinit.add(1)

    last = [(sinit, sinit.n)]

    def sec(engine, emitfn, extra_wait=None):
        @getattr(block, engine)
        def _(e):
            for w in (extra_wait or []):
                e.wait_ge(w[0].h, w[1])
            e.wait_ge(last[0][0].h, last[0][1])
            ins = emitfn(e)
            ins.then_inc(ust.h, 1)
        last[0] = (ust, ust.add(1))

    for blk in range(NLB):
        @block.sync
        def _(sy, blk=blk):
            sy.wait_ge(last[0][0].h, last[0][1])
            sy.dma_start(out=qblk[:, 0:3 * D],
                         in_=qkv_h[ts(2 * blk), :]).then_inc(blkld.h, 16)
            sy.dma_start(out=qblk[:, 3 * D:6 * D],
                         in_=qkv_h[ts(2 * blk + 1), :]).then_inc(blkld.h, 16)
        blkld.add(32)

        for h in range(H):
            def v1(v, blk=blk, h=h):
                ins = None
                for half in range(2):
                    q_ap = qblk[:, half * 3 * D + h * DH:
                                half * 3 * D + (h + 1) * DH]
                    k_ap = qblk[:, half * 3 * D + D + h * DH:
                                half * 3 * D + D + (h + 1) * DH]
                    v.tensor_scalar(qt[:, half * DH:(half + 1) * DH], q_ap,
                                    dec[:, h * 6 + half:h * 6 + half + 1],
                                    None, ALU.mult)
                    v.drain()
                    v.tensor_scalar(kit[:, half * DH:(half + 1) * DH], k_ap,
                                    dec[:, h * 6 + 4 + half:h * 6 + 5 + half],
                                    None, ALU.mult)
                    v.drain()
                    v.tensor_scalar(ktmp[:, half * DH:(half + 1) * DH], k_ap,
                                    dec[:, h * 6 + 2 + half:h * 6 + 3 + half],
                                    None, ALU.mult)
                    v.drain()
                ins = v.tensor_copy(Sbf[:], Ssb[:, h * DH:(h + 1) * DH])
                return ins
            sec("vector", v1, extra_wait=[(blkld, blkld.n), (decd, decd.n),
                                          (adbg, 32 * blk)])

            emit_transpose(p, block,
                           tiles=[(qt[:, 0:DH], qfm[:, 0:DH]),
                                  (qt[:, DH:2 * DH], qfm[:, DH:2 * DH]),
                                  (kit[:, 0:DH], kfm[:, 0:DH]),
                                  (kit[:, DH:2 * DH], kfm[:, DH:2 * DH])],
                           ident=ident, pt=ptT, evac="vector",
                           wait_t=[last[0], (ld, 16)], sems=tsem)
            last[0] = (tsem[1], tsem[1].n)

            def t2(t, blk=blk, h=h):
                t.matmul(psSa[:], kfm[:, 0:DH], qfm[:, 0:2 * DH],
                         start=True, stop=True)
                return t.matmul(psSb[:], kfm[:, DH:2 * DH], qfm[:, 0:2 * DH],
                                start=True, stop=True)
            sec("tensor", t2)

            def v2(v, blk=blk, h=h):
                v.tensor_tensor(Pk0[:, 0:DH], psSa[:, 0:DH], mask01[:],
                                ALU.mult)
                v.drain()
                v.tensor_copy(Pk0[:, DH:2 * DH], psSa[:, DH:2 * DH])
                v.drain()
                return v.tensor_tensor(Pk1[:, DH:2 * DH], psSb[:, DH:2 * DH],
                                       mask01[:], ALU.mult)
            sec("vector", v2)

            def t3a(t, blk=blk, h=h):
                t.matmul(psX[:], qfm[:, 0:DH], Sbf[:], start=True, stop=False)
                return t.matmul(psX[:], Pk0[:, 0:DH],
                                qblk[:, 2 * D + h * DH:2 * D + (h + 1) * DH],
                                start=False, stop=True)
            sec("tensor", t3a)

            def v3a(v, blk=blk, h=h):
                return v.tensor_copy(
                    attn_sb[:, h * DH:(h + 1) * DH], psX[:])
            sec("vector", v3a)

            def t3b(t, blk=blk, h=h):
                t.matmul(psX[:], qfm[:, DH:2 * DH], Sbf[:],
                         start=True, stop=False)
                t.matmul(psX[:], Pk0[:, DH:2 * DH],
                         qblk[:, 2 * D + h * DH:2 * D + (h + 1) * DH],
                         start=False, stop=False)
                return t.matmul(
                    psX[:], Pk1[:, DH:2 * DH],
                    qblk[:, 3 * D + 2 * D + h * DH:3 * D + 2 * D + (h + 1) * DH],
                    start=False, stop=True)
            sec("tensor", t3b)

            def v3b(v, blk=blk, h=h):
                return v.tensor_copy(
                    attn_sb[:, D + h * DH:D + (h + 1) * DH], psX[:])
            sec("vector", v3b)

            def t3c(t, blk=blk, h=h):
                t.matmul(psX[:], ktmp[:, 0:DH],
                         qblk[:, 2 * D + h * DH:2 * D + (h + 1) * DH],
                         start=True, stop=False)
                return t.matmul(
                    psX[:], ktmp[:, DH:2 * DH],
                    qblk[:, 3 * D + 2 * D + h * DH:3 * D + 2 * D + (h + 1) * DH],
                    start=False, stop=True)
            sec("tensor", t3c)

            def v3c(v, blk=blk, h=h):
                return v.scalar_tensor_tensor(
                    Ssb[:, h * DH:(h + 1) * DH], Ssb[:, h * DH:(h + 1) * DH],
                    bd[:, h:h + 1], psX[:], ALU.mult, ALU.add)
            sec("vector", v3c)

        @block.sync
        def _(sy, blk=blk):
            sy.wait_ge(last[0][0].h, last[0][1])
            for mh in range(2):
                sy.dma_start(out=attn_dbg[ts(2 * blk + mh), :],
                             in_=attn_sb[:, mh * D:(mh + 1) * D]
                             ).then_inc(adbg.h, 16)
        adbg.add(32)

        for mh in range(2):
            m = 2 * blk + mh

            def r1(s, mh=mh):
                return s.activation(sqs[:], attn_sb[:, mh * D:(mh + 1) * D],
                                    ACT.Square, accum_out=stat[:, 0:1])
            sec("scalar", r1)

            def r2(v, mh=mh):
                return v.tensor_scalar(stat[:, 1:2], stat[:, 0:1], 1.0 / D,
                                       1e-6, ALU.mult, ALU.add)
            sec("vector", r2)

            def r3(s, mh=mh):
                return s.activation(stat[:, 2:3], stat[:, 1:2], ACT.Sqrt)
            sec("scalar", r3)

            def r4(v, mh=mh, m=m):
                v.reciprocal(stat[:, 3:4], stat[:, 2:3])
                v.drain()
                v.scalar_tensor_tensor(stat[:, 4:5], stat[:, 3:4], 1.0,
                                       stat[:, 3:4], ALU.mult, ALU.mult)
                v.drain()
                v.scalar_tensor_tensor(stat[:, 4:5], stat[:, 4:5], -0.5,
                                       stat[:, 1:2], ALU.mult, ALU.mult)
                v.drain()
                v.tensor_scalar(stat[:, 4:5], stat[:, 4:5], 1.5, None,
                                ALU.add)
                v.drain()
                v.tensor_tensor(stat[:, 3:4], stat[:, 3:4], stat[:, 4:5],
                                ALU.mult)
                v.drain()
                v.tensor_tensor(gate_sb[:, m * D:(m + 1) * D],
                                attn_sb[:, mh * D:(mh + 1) * D],
                                gate_sb[:, m * D:(m + 1) * D], ALU.mult)
                v.drain()
                return v.scalar_tensor_tensor(
                    gate_sb[:, m * D:(m + 1) * D],
                    gate_sb[:, m * D:(m + 1) * D],
                    stat[:, 3:4], normw[:], ALU.mult, ALU.mult)
            sec("vector", r4)

            emit_transpose(
                p, block,
                tiles=[(gate_sb[:, m * D + c * 128:m * D + (c + 1) * 128],
                        gatedT[:, c * TOK + m * 128:c * TOK + (m + 1) * 128])
                       for c in range(D // 128)],
                ident=ident, pt=ptT, evac="vector",
                wait_t=[last[0]], sems=tsem)
            last[0] = (tsem[1], tsem[1].n)


    def oev(e, m, n, ps, idx):
        return e.tensor_tensor(x_sb[:, m * D + n * 256:m * D + (n + 1) * 256],
                               ps,
                               x_sb[:, m * D + n * 256:m * D + (n + 1) * 256],
                               ALU.add)

    o_ev = gemm_tm(p, block,
                   lhsT_tile=lambda k, m: gatedT[:, k * TOK + m * 128:
                                                 k * TOK + (m + 1) * 128],
                   k_chunks=D // 128, n_chunks=D // 256, nb=256,
                   panels=panels,
                   panel_srcs=lambda n: [wout_h[ts(k), n * 256:(n + 1) * 256]
                                         for k in range(D // 128)],
                   cast=True, evac_engine="vector", evac_emit=oev,
                   wait_t=[last[0]], psA=psA, psB=psB)

    xst = p.sem()

    @block.sync
    def _(sy):
        sy.wait_ge(o_ev.h, o_ev.n)
        for m in range(TT):
            sy.dma_start(out=x1_o[ts(m), :],
                         in_=x_sb[:, m * D:(m + 1) * D]).then_inc(xst.h, 16)
    xst.add(16 * TT)

    @block.gpsimd
    def _(g):
        g.wait_ge(xst.h, xst.n)
        g.wait_ge(adbg.h, adbg.n)

    return p


def _mask01():
    import ml_dtypes
    k = np.arange(128)[:, None]
    m = np.arange(128)[None, :]
    return (k <= m).astype(ml_dtypes.bfloat16)


def run_p2(inputs, p1_results):
    x = np.asarray(inputs["L0_attn_hidden_states"], dtype=np.float32)
    woutT = np.ascontiguousarray(
        np.asarray(inputs["L0_attn_w_out"], dtype=np.float32).T)
    normw = np.ascontiguousarray(np.asarray(
        inputs["L0_attn_norm_w"], dtype=np.float32).reshape(1, D))
    slope = np.ascontiguousarray(np.asarray(
        inputs["L0_attn_slope_rate"], dtype=np.float32).reshape(1, H))
    shards = _shard_tokens(x)
    Ls = [np.asarray(r["L"]) for r in p1_results]
    p = build_p2()
    in_maps = []
    for c in range(NCORES):
        j0 = c % 4
        g = (c // 4) * 4
        Lprev = np.zeros((3 * 128, H * DH), np.float32)
        dist = np.full((1, 3), 1e6, np.float32)
        for j in range(j0):
            Lprev[(j) * 128:(j + 1) * 128] = Ls[g + j]
            dist[0, j] = float(j0 - 1 - j)
        in_maps.append({
            "x": shards[c], "qkv": np.asarray(p1_results[c]["qkv"]),
            "Lall": np.concatenate(Ls, axis=0),
            "gate": np.asarray(p1_results[c]["gate"]),
            "woutT": woutT, "normw": normw, "slope": slope,
            "iota6": _iota6(), "ident": _ident_bf16(),
            "mask01": _mask01(),
            "jprev": np.zeros((1, 4), np.float32),
            "Lprev": Lprev, "dist": dist,
        })
    res = run_bass_kernel_spmd(p.nc, in_maps, core_ids=list(range(NCORES)))
    return res.results



def emit_mx_rowtile(p, block, *, wrow, K, bm, sc, r2, q8, wq, chain, sec,
                    extra_wait=None):
    nb32 = K // 32

    def v_bm(v):
        return v.tensor_reduce(bm[:, 0:nb32],
                               wrow.rearrange("p (b k) -> p b k", k=32),
                               AX.X, ALU.max, apply_absolute_value=True)
    sec("vector", v_bm, extra_wait=extra_wait)

    def v_sc(v):
        v.tensor_scalar(sc[:, 0:nb32], bm[:, 0:nb32], 1.0 / 448.0, 1e-30,
                        ALU.mult, ALU.max)
        v.drain()
        v.tensor_scalar(sc[:, 0:nb32], sc[:, 0:nb32], 2.0, None, ALU.mult)
        v.drain()
        return v.reciprocal(r2[:, 0:nb32], sc[:, 0:nb32])
    sec("vector", v_sc)

    def v_q8(v):
        return v.tensor_tensor(
            q8[:, 0:K].rearrange("p (b k) -> p b k", k=32),
            wrow.rearrange("p (b k) -> p b k", k=32),
            r2[:, 0:nb32].unsqueeze(2).broadcast_to([128, nb32, 32]),
            ALU.mult)
    sec("vector", v_q8)

    def v_dq(v):
        return v.tensor_tensor(
            wq[:, 0:K].rearrange("p (b k) -> p b k", k=32),
            q8[:, 0:K].rearrange("p (b k) -> p b k", k=32),
            sc[:, 0:nb32].unsqueeze(2).broadcast_to([128, nb32, 32]),
            ALU.mult)
    sec("vector", v_dq)


def make_sec(p, block, ust, last):
    def sec(engine, emitfn, extra_wait=None):
        @getattr(block, engine)
        def _(e):
            for w in (extra_wait or []):
                e.wait_ge(w[0].h, w[1])
            e.wait_ge(last[0][0].h, last[0][1])
            ins = emitfn(e)
            ins.then_inc(ust.h, 1)
        last[0] = (ust, ust.add(1))
    return sec



def build_w1():
    p = Prog("w1")
    g_h = p.din("g0", [I_FFN // NCORES, D])
    u_h = p.din("u0", [I_FFN // NCORES, D])
    d_h = p.din("d0", [D // NCORES, I_FFN])
    g1_h = p.din("g1", [I_FFN // NCORES, D])
    u1_h = p.din("u1", [I_FFN // NCORES, D])
    d1_h = p.din("d1", [D // NCORES, I_FFN])
    ident_h = p.din("ident", [128, 128], BF16)
    gT_o = p.dout("gT", [D, I_FFN // NCORES], BF16)
    uT_o = p.dout("uT", [D, I_FFN // NCORES], BF16)
    dT_o = p.dout("dT", [I_FFN, D // NCORES], BF16)
    part_o = p.dout("part", [128, 3])

    wrow = p.sbuf("wrow", [128, I_FFN], F32)
    q8 = p.sbuf("q8", [128, I_FFN], FP8)
    wq = p.sbuf("wq", [128, I_FFN], BF16)
    bm = p.sbuf("bm", [128, 256], F32)
    sc = p.sbuf("sc", [128, 256], F32)
    r2 = p.sbuf("r2", [128, 256], F32)
    gTs = p.sbuf("gTs", [128, (D // 128) * (I_FFN // NCORES)], BF16)
    uTs = p.sbuf("uTs", [128, (D // 128) * (I_FFN // NCORES)], BF16)
    dTs = p.sbuf("dTs", [128, (I_FFN // 128) * (D // NCORES)], BF16)
    part = p.sbuf("partS", [128, 4], F32)
    red = p.sbuf("red", [128, 1], F32)
    ident = p.sbuf("identS", [128, 128], BF16)
    ptT = (p.psum("ptTa", [128, 128], BF16), p.psum("ptTb", [128, 128], BF16))

    block = p.block()
    ld = p.sem()
    tld = p.sem()
    ust = p.sem()
    tsem = (p.sem(), p.sem())
    std = p.sem()
    last = [(ld, 0)]
    sec = make_sec(p, block, ust, last)

    @block.sync
    def _(sy):
        sy.dma_start(out=ident[:], in_=ident_h).then_inc(ld.h, 16)
    ld.add(16)

    def vz(v):
        return v.memset(part[:], 0.0)
    sec("vector", vz)

    jobs = [(g_h, gTs, I_FFN // NCORES, D),
            (u_h, uTs, I_FFN // NCORES, D),
            (d_h, dTs, D // NCORES, I_FFN)]
    for src_h, dstT, rows, K in jobs:
        ncol = K // 128
        for rt in range(rows // 128):
            @block.sync
            def _(sy, src_h=src_h, rt=rt, K=K):
                sy.wait_ge(last[0][0].h, last[0][1])
                sy.dma_start(out=wrow[:, 0:K],
                             in_=src_h[ts(rt), :]).then_inc(tld.h, 16)
            tld.add(16)
            lw = (tld, tld.n)
            emit_mx_rowtile(p, block, wrow=wrow[:, 0:K], K=K, bm=bm, sc=sc,
                            r2=r2, q8=q8, wq=wq, chain=last, sec=sec,
                            extra_wait=[lw])

            emit_transpose(
                p, block,
                tiles=[(wq[:, c * 128:(c + 1) * 128],
                        dstT[:, c * (rows) + rt * 128:
                             c * (rows) + (rt + 1) * 128])
                       for c in range(ncol)],
                ident=ident, pt=ptT, evac="vector",
                wait_t=[last[0], (ld, ld.n)], sems=tsem)
            last[0] = (tsem[1], tsem[1].n)

    pjobs = [(g1_h, 0, I_FFN // NCORES, D), (u1_h, 1, I_FFN // NCORES, D),
             (d1_h, 2, D // NCORES, I_FFN)]
    for src_h, col, rows, K in pjobs:
        for rt in range(rows // 128):
            @block.sync
            def _(sy, src_h=src_h, rt=rt, K=K):
                sy.wait_ge(last[0][0].h, last[0][1])
                sy.dma_start(out=wrow[:, 0:K],
                             in_=src_h[ts(rt), :]).then_inc(tld.h, 16)
            tld.add(16)
            lw = (tld, tld.n)

            def v_red(v, K=K):
                return v.tensor_reduce(red[:], wrow[:, 0:K], AX.X, ALU.add,
                                       apply_absolute_value=True)
            sec("vector", v_red, extra_wait=[lw])

            def v_acc(v, col=col):
                return v.tensor_tensor(part[:, col:col + 1],
                                       part[:, col:col + 1], red[:], ALU.add)
            sec("vector", v_acc)

    @block.sync
    def _(sy):
        sy.wait_ge(last[0][0].h, last[0][1])
        for c in range(D // 128):
            sy.dma_start(out=gT_o[ts(c), :],
                         in_=gTs[:, c * (I_FFN // NCORES):
                                 (c + 1) * (I_FFN // NCORES)]
                         ).then_inc(std.h, 16)
            sy.dma_start(out=uT_o[ts(c), :],
                         in_=uTs[:, c * (I_FFN // NCORES):
                                 (c + 1) * (I_FFN // NCORES)]
                         ).then_inc(std.h, 16)
        for c in range(I_FFN // 128):
            sy.dma_start(out=dT_o[ts(c), :],
                         in_=dTs[:, c * (D // NCORES):
                                 (c + 1) * (D // NCORES)]).then_inc(std.h, 16)
        sy.dma_start(out=part_o, in_=part[:, 0:3]).then_inc(std.h, 16)
    std.add(16 * (2 * (D // 128) + I_FFN // 128 + 1))

    @block.gpsimd
    def _(g):
        g.wait_ge(std.h, std.n)

    return p


def run_w1(inputs):
    sh = I_FFN // NCORES
    shd = D // NCORES
    g0 = np.asarray(inputs["L0_ffn_gate_w"], np.float32)
    u0 = np.asarray(inputs["L0_ffn_up_w"], np.float32)
    d0 = np.asarray(inputs["L0_ffn_down_w"], np.float32)
    g1 = np.asarray(inputs["L1_ffn_w_gate"], np.float32)
    u1 = np.asarray(inputs["L1_ffn_w_up"], np.float32)
    d1 = np.asarray(inputs["L1_ffn_w_down"], np.float32)
    p = build_w1()
    in_maps = [{"g0": np.ascontiguousarray(g0[c * sh:(c + 1) * sh]),
                "u0": np.ascontiguousarray(u0[c * sh:(c + 1) * sh]),
                "d0": np.ascontiguousarray(d0[c * shd:(c + 1) * shd]),
                "g1": np.ascontiguousarray(g1[c * sh:(c + 1) * sh]),
                "u1": np.ascontiguousarray(u1[c * sh:(c + 1) * sh]),
                "d1": np.ascontiguousarray(d1[c * shd:(c + 1) * shd]),
                "ident": _ident_bf16()}
               for c in range(NCORES)]
    res = run_bass_kernel_spmd(p.nc, in_maps, core_ids=list(range(NCORES)))
    return res.results



def build_w2():
    p = Prog("w2")
    g1_h = p.din("g1", [I_FFN // NCORES, D])
    u1_h = p.din("u1", [I_FFN // NCORES, D])
    d1_h = p.din("d1", [D // NCORES, I_FFN])
    parts_h = p.din("parts", [NCORES * 128, 3])
    ident_h = p.din("ident", [128, 128], BF16)
    ones_h = p.din("ones128", [128, 1])
    onesr_h = p.din("onesrow", [1, 128])
    gT_o = p.dout("gT", [D, I_FFN // NCORES], BF16)
    uT_o = p.dout("uT", [D, I_FFN // NCORES], BF16)
    dT_o = p.dout("dT", [I_FFN, D // NCORES], BF16)

    wrow = p.sbuf("wrow", [128, I_FFN], F32)
    wq = p.sbuf("wq", [128, I_FFN], BF16)
    gTs = p.sbuf("gTs", [128, (D // 128) * (I_FFN // NCORES)], BF16)
    uTs = p.sbuf("uTs", [128, (D // 128) * (I_FFN // NCORES)], BF16)
    dTs = p.sbuf("dTs", [128, (I_FFN // 128) * (D // NCORES)], BF16)
    partsb = p.sbuf("partsb", [128, 24], F32)
    p3 = p.sbuf("p3S", [128, 3], F32)
    tot = p.sbuf("totS", [128, 3], F32)
    smean = p.sbuf("smean", [128, 8], F32)
    ident = p.sbuf("identS", [128, 128], BF16)
    ones = p.sbuf("onesS", [128, 1], F32)
    onesr = p.sbuf("onesrS", [1, 128], F32)
    ptT = (p.psum("ptTa", [128, 128], BF16), p.psum("ptTb", [128, 128], BF16))
    psT = p.psum("psT", [128, 128], F32)

    block = p.block()
    ld = p.sem()
    tld = p.sem()
    ust = p.sem()
    tsem = (p.sem(), p.sem())
    std = p.sem()
    last = [(ld, 0)]
    sec = make_sec(p, block, ust, last)

    @block.sync
    def _(sy):
        sy.dma_start(out=ident[:], in_=ident_h).then_inc(ld.h, 16)
        sy.dma_start(out=ones[:], in_=ones_h).then_inc(ld.h, 16)
        sy.dma_start(out=onesr[:], in_=onesr_h).then_inc(ld.h, 16)
        sy.dma_start(out=partsb[:].rearrange("p (j c) -> p j c", c=3),
                     in_=parts_h.rearrange("(j p) c -> p j c", p=128)
                     ).then_inc(ld.h, 16)
    ld.add(64)

    def v_red(v):
        return v.tensor_reduce(p3[:],
                               partsb[:].rearrange("p (j c) -> p c j", c=3),
                               AX.X, ALU.add)
    sec("vector", v_red, extra_wait=[(ld, 64)])

    def t_tot(t):
        return t.matmul(psT[0:1, 0:3], ones[:], p3[:], start=True, stop=True)
    sec("tensor", t_tot)

    def v_tot(v):
        return v.tensor_copy(tot[0:1, 0:3], psT[0:1, 0:3])
    sec("vector", v_tot)

    def t_bc(t):
        return t.matmul(psT[:, 0:3], onesr[:], tot[0:1, 0:3],
                        start=True, stop=True)
    sec("tensor", t_bc)

    def v_sc(v):
        v.tensor_scalar(smean[:, 3:6], psT[:, 0:3], 1.0 / 16777216.0, 1e-5,
                        ALU.mult, ALU.max)
        v.drain()
        return v.reciprocal(smean[:, 0:3], smean[:, 3:6])
    sec("vector", v_sc)

    jobs = [(g1_h, gTs, 0, I_FFN // NCORES, D),
            (u1_h, uTs, 1, I_FFN // NCORES, D),
            (d1_h, dTs, 2, D // NCORES, I_FFN)]
    for src_h, dstT, col, rows, K in jobs:
        ncol = K // 128
        for rt in range(rows // 128):
            @block.sync
            def _(sy, src_h=src_h, rt=rt, K=K):
                sy.wait_ge(last[0][0].h, last[0][1])
                sy.dma_start(out=wrow[:, 0:K],
                             in_=src_h[ts(rt), :]).then_inc(tld.h, 16)
            tld.add(16)
            lw = (tld, tld.n)

            def v_q(v, col=col, K=K):
                v.tensor_scalar(wrow[:, 0:K], wrow[:, 0:K],
                                smean[:, col:col + 1], MAGIC,
                                ALU.mult, ALU.add)
                v.drain()
                v.tensor_scalar(wrow[:, 0:K], wrow[:, 0:K], MAGIC, 1.0,
                                ALU.subtract, ALU.min)
                v.drain()
                v.tensor_scalar(wrow[:, 0:K], wrow[:, 0:K], -1.0, None,
                                ALU.max)
                v.drain()
                return v.tensor_scalar(wq[:, 0:K], wrow[:, 0:K],
                                       smean[:, col + 3:col + 4], None,
                                       ALU.mult)
            sec("vector", v_q, extra_wait=[lw])

            emit_transpose(
                p, block,
                tiles=[(wq[:, c * 128:(c + 1) * 128],
                        dstT[:, c * rows + rt * 128:c * rows + (rt + 1) * 128])
                       for c in range(ncol)],
                ident=ident, pt=ptT, evac="vector",
                wait_t=[last[0], (ld, 16)], sems=tsem)
            last[0] = (tsem[1], tsem[1].n)

    @block.sync
    def _(sy):
        sy.wait_ge(last[0][0].h, last[0][1])
        for c in range(D // 128):
            sy.dma_start(out=gT_o[ts(c), :],
                         in_=gTs[:, c * (I_FFN // NCORES):
                                 (c + 1) * (I_FFN // NCORES)]
                         ).then_inc(std.h, 16)
            sy.dma_start(out=uT_o[ts(c), :],
                         in_=uTs[:, c * (I_FFN // NCORES):
                                 (c + 1) * (I_FFN // NCORES)]
                         ).then_inc(std.h, 16)
        for c in range(I_FFN // 128):
            sy.dma_start(out=dT_o[ts(c), :],
                         in_=dTs[:, c * (D // NCORES):
                                 (c + 1) * (D // NCORES)]).then_inc(std.h, 16)
    std.add(16 * (2 * (D // 128) + I_FFN // 128))

    @block.gpsimd
    def _(g):
        g.wait_ge(std.h, std.n)

    return p


def run_w2(inputs, w1_results):
    sh = I_FFN // NCORES
    shd = D // NCORES
    g1 = np.asarray(inputs["L1_ffn_w_gate"], np.float32)
    u1 = np.asarray(inputs["L1_ffn_w_up"], np.float32)
    d1 = np.asarray(inputs["L1_ffn_w_down"], np.float32)
    parts = np.concatenate([np.asarray(r["part"]) for r in w1_results], axis=0)
    p = build_w2()
    in_maps = [{"g1": np.ascontiguousarray(g1[c * sh:(c + 1) * sh]),
                "u1": np.ascontiguousarray(u1[c * sh:(c + 1) * sh]),
                "d1": np.ascontiguousarray(d1[c * shd:(c + 1) * shd]),
                "parts": parts, "ident": _ident_bf16(),
                "ones128": np.ones((128, 1), np.float32),
                "onesrow": np.ones((1, 128), np.float32)}
               for c in range(NCORES)]
    res = run_bass_kernel_spmd(p.nc, in_maps, core_ids=list(range(NCORES)))
    return res.results



IC = 1024
NIC = I_FFN // IC


def build_p3():
    p = Prog("p3")
    x_h = p.din("x", [TOK, D])
    gT_h = p.din("gTa", [D, I_FFN], BF16)
    uT_h = p.din("uTa", [D, I_FFN], BF16)
    dT_h = p.din("dTa", [I_FFN, D], BF16)
    ident_h = p.din("ident", [128, 128], BF16)
    x2_o = p.dout("x2", [TOK, D])

    x_sb = p.sbuf("x_sb", [128, TT * D], F32)
    xqT = p.sbuf("xqT", [128, (D // 128) * TOK], BF16)
    hbuf = p.sbuf("hbuf", [128, TT * IC], F32)
    hqT = p.sbuf("hqT", [128, (IC // 128) * TOK], BF16)
    xln_t = p.sbuf("xln_t", [128, D], F32)
    xq_t = p.sbuf("xq_t", [128, D], BF16)
    q8t = p.sbuf("q8t", [128, D], FP8)
    bm = p.sbuf("bm", [128, 64], F32)
    sc = p.sbuf("sc", [128, 64], F32)
    r2 = p.sbuf("r2", [128, 64], F32)
    hq_t = xq_t[:, 0:IC]
    q8h = q8t[:, 0:IC]
    stat = p.sbuf("stat", [128, 8], F32)
    scr6 = p.sbuf("scr6", [128, 24], F32)
    ident = p.sbuf("identS", [128, 128], BF16)
    panels = Panels(p, "wpan", D // 128, 512)
    panelsD = Panels(p, "wpanD", IC // 128, 512)

    psA = p.psum("psA", [128, 512], F32)
    psB = p.psum("psB", [128, 512], F32)
    ptT = (p.psum("ptTa", [128, 128], BF16), p.psum("ptTb", [128, 128], BF16))

    block = p.block()
    ld = p.sem()
    xload = p.sem()
    ust = p.sem()
    tsem = (p.sem(), p.sem())
    std = p.sem()
    last = [(ld, 0)]
    sec = make_sec(p, block, ust, last)

    @block.sync
    def _(sy):
        sy.dma_start(out=ident[:], in_=ident_h).then_inc(ld.h, 16)
        for m in range(TT):
            sy.dma_start(out=x_sb[:, m * D:(m + 1) * D],
                         in_=x_h[ts(m), :]).then_inc(xload.h, 16)
    ld.add(16)
    xload.add(16 * TT)

    for m in range(TT):
        emit_ln(p, block, x_ap=x_sb[:, m * D:(m + 1) * D], out_ap=xln_t[:],
                stat=stat, scr6=scr6,
                wait_v=[last[0], (xload, 16 * (m + 1))], done=ust)
        last[0] = (ust, ust.n)

        emit_mx_rowtile(p, block, wrow=xln_t[:], K=D, bm=bm, sc=sc, r2=r2,
                        q8=q8t, wq=xq_t, chain=last, sec=sec)

        emit_transpose(
            p, block,
            tiles=[(xq_t[:, c * 128:(c + 1) * 128],
                    xqT[:, c * TOK + m * 128:c * TOK + (m + 1) * 128])
                   for c in range(D // 128)],
            ident=ident, pt=ptT, evac="vector",
            wait_t=[last[0], (ld, 16)], sems=tsem)
        last[0] = (tsem[1], tsem[1].n)

    prev_ev = [None]

    for ic in range(NIC):
        hold = last[0]

        def silu_evac(e, m, n, ps, idx, hold=hold):
            if idx == 0:
                e.wait_ge(hold[0].h, hold[1])
            return e.activation(hbuf[:, m * IC + n * 512:m * IC + (n + 1) * 512],
                                ps, ACT.Silu)

        g_ev = gemm_tm(p, block,
                       lhsT_tile=lambda k, m: xqT[:, k * TOK + m * 128:
                                                  k * TOK + (m + 1) * 128],
                       k_chunks=D // 128, n_chunks=IC // 512, nb=512,
                       panels=panels,
                       panel_srcs=lambda n, ic=ic: [
                           gT_h[ts(k), ic * IC + n * 512:ic * IC + (n + 1) * 512]
                           for k in range(D // 128)],
                       cast=False, evac_engine="scalar", evac_emit=silu_evac,
                       wait_t=([last[0]] if prev_ev[0] is None
                               else [last[0], prev_ev[0]]),
                       psA=psA, psB=psB)

        def up_evac(e, m, n, ps, idx):
            return e.tensor_tensor(
                hbuf[:, m * IC + n * 512:m * IC + (n + 1) * 512], ps,
                hbuf[:, m * IC + n * 512:m * IC + (n + 1) * 512], ALU.mult)

        u_ev = gemm_tm(p, block,
                       lhsT_tile=lambda k, m: xqT[:, k * TOK + m * 128:
                                                  k * TOK + (m + 1) * 128],
                       k_chunks=D // 128, n_chunks=IC // 512, nb=512,
                       panels=panels,
                       panel_srcs=lambda n, ic=ic: [
                           uT_h[ts(k), ic * IC + n * 512:ic * IC + (n + 1) * 512]
                           for k in range(D // 128)],
                       cast=False, evac_engine="vector", evac_emit=up_evac,
                       wait_t=[(g_ev, g_ev.n)], psA=psA, psB=psB)

        last[0] = (u_ev, u_ev.n)
        for m in range(TT):
            emit_mx_rowtile(p, block, wrow=hbuf[:, m * IC:(m + 1) * IC],
                            K=IC, bm=bm, sc=sc, r2=r2, q8=q8h, wq=hq_t,
                            chain=last, sec=sec)
            emit_transpose(
                p, block,
                tiles=[(hq_t[:, c * 128:(c + 1) * 128],
                        hqT[:, c * TOK + m * 128:c * TOK + (m + 1) * 128])
                       for c in range(IC // 128)],
                ident=ident, pt=ptT, evac="vector",
                wait_t=[last[0]], sems=tsem)
            last[0] = (tsem[1], tsem[1].n)

        def d_evac(e, m, n, ps, idx):
            return e.tensor_tensor(
                x_sb[:, m * D + n * 512:m * D + (n + 1) * 512], ps,
                x_sb[:, m * D + n * 512:m * D + (n + 1) * 512], ALU.add)

        d_ev = gemm_tm(p, block,
                       lhsT_tile=lambda k, m: hqT[:, k * TOK + m * 128:
                                                  k * TOK + (m + 1) * 128],
                       k_chunks=IC // 128, n_chunks=D // 512, nb=512,
                       panels=panelsD,
                       panel_srcs=lambda n, ic=ic: [
                           dT_h[ic * IC + k * 128:ic * IC + (k + 1) * 128,
                                n * 512:(n + 1) * 512]
                           for k in range(IC // 128)],
                       cast=False, evac_engine="vector", evac_emit=d_evac,
                       wait_t=[last[0]], psA=psA, psB=psB)
        prev_ev[0] = (d_ev, d_ev.n)
        last[0] = (d_ev, d_ev.n)

    @block.sync
    def _(sy):
        sy.wait_ge(last[0][0].h, last[0][1])
        for m in range(TT):
            sy.dma_start(out=x2_o[ts(m), :],
                         in_=x_sb[:, m * D:(m + 1) * D]).then_inc(std.h, 16)
    std.add(16 * TT)

    @block.gpsimd
    def _(g):
        g.wait_ge(std.h, std.n)

    return p


def run_p3(inputs, x1_shards, w1_results):
    gTa = np.concatenate([np.asarray(r["gT"]) for r in w1_results], axis=1)
    uTa = np.concatenate([np.asarray(r["uT"]) for r in w1_results], axis=1)
    dTa = np.concatenate([np.asarray(r["dT"]) for r in w1_results], axis=1)
    p = build_p3()
    in_maps = [{"x": x1_shards[c], "gTa": gTa, "uTa": uTa, "dTa": dTa,
                "ident": _ident_bf16()} for c in range(NCORES)]
    res = run_bass_kernel_spmd(p.nc, in_maps, core_ids=list(range(NCORES)))
    return res.results



def build_p4():
    p = Prog("p4")
    x_h = p.din("x", [TOK, D])
    wkv_h = p.din("wkv", [D, KVL])
    wq_h = p.din("wq", [D, QL])
    wkr_h = p.din("wkr", [D, RD])
    cos_h = p.din("cosw", [TOK, RD])
    sgnsin_h = p.din("sgnsin", [TOK, RD])
    ident_h = p.din("ident", [128, 128], BF16)
    out_o = p.dout("cqkvT", [QL + KVL + RD, TOK], BF16)

    x_sb = p.sbuf("x_sb", [128, TT * D], F32)
    xlnT = p.sbuf("xlnT", [128, (D // 128) * TOK], BF16)
    xln = p.sbuf("xln", [128, D], BF16)
    fmbuf = p.sbuf("fmbuf", [128, 16 * TOK], BF16)
    krT = p.sbuf("krT", [128, TOK], BF16)
    krt = p.sbuf("krt", [128, RD], BF16)
    krr = p.sbuf("krr", [128, RD], BF16)
    krf = p.sbuf("krf", [128, 2 * RD], F32)
    cosb = p.sbuf("cosb", [128, TT * RD], F32)
    sgnb = p.sbuf("sgnb", [128, TT * RD], F32)
    wbuf = p.sbuf("wbuf", [128, 16 * 128], BF16)
    stat = p.sbuf("stat", [128, 8], F32)
    scr6 = p.sbuf("scr6", [128, 24], F32)
    ident = p.sbuf("identS", [128, 128], BF16)

    psA = p.psum("psA", [128, 512], F32)
    psB = p.psum("psB", [128, 512], F32)
    ptT = (p.psum("ptTa", [128, 128], BF16), p.psum("ptTb", [128, 128], BF16))

    block = p.block()
    ld = p.sem()
    gld = p.sem()
    xload = p.sem()
    ust = p.sem()
    tsem = (p.sem(), p.sem())
    std = p.sem()
    last = [(ld, 0)]
    sec = make_sec(p, block, ust, last)

    @block.sync
    def _(sy):
        sy.dma_start(out=ident[:], in_=ident_h).then_inc(ld.h, 16)
        sy.dma_start(out=cosb[:].rearrange("p (m r) -> p m r", r=RD),
                     in_=cos_h.rearrange("(m p) r -> p m r", p=128)
                     ).then_inc(ld.h, 16)
        sy.dma_start(out=sgnb[:].rearrange("p (m r) -> p m r", r=RD),
                     in_=sgnsin_h.rearrange("(m p) r -> p m r", p=128)
                     ).then_inc(ld.h, 16)
        for m in range(TT):
            sy.dma_start(out=x_sb[:, m * D:(m + 1) * D],
                         in_=x_h[ts(m), :]).then_inc(xload.h, 16)
    ld.add(48)
    xload.add(16 * TT)

    for m in range(TT):
        emit_ln(p, block, x_ap=x_sb[:, m * D:(m + 1) * D], out_ap=xln[:],
                stat=stat, scr6=scr6,
                wait_v=[last[0], (xload, 16 * (m + 1))], done=ust)
        last[0] = (ust, ust.n)

        emit_transpose(
            p, block,
            tiles=[(xln[:, c * 128:(c + 1) * 128],
                    xlnT[:, c * TOK + m * 128:c * TOK + (m + 1) * 128])
                   for c in range(D // 128)],
            ident=ident, pt=ptT, evac="vector",
            wait_t=[last[0], (ld, 16)], sems=tsem)
        last[0] = (tsem[1], tsem[1].n)

    def load_wchunk(w_h, oc, width):
        @block.gpsimd
        def _(g, w_h=w_h, oc=oc, width=width):
            g.wait_ge(last[0][0].h, last[0][1])
            for k in range(D // 128):
                g.dma_start(out=wbuf[:, k * 128:k * 128 + width],
                            in_=w_h[ts(k), oc * width:(oc + 1) * width]
                            ).then_inc(gld.h, 16)
        gld.add(16 * (D // 128))
        return (gld, gld.n)

    wjobs = [(wq_h, 0, QL // 128), (wkv_h, 12, KVL // 128)]
    for w_h, oc0, noc in wjobs:
        for oc in range(noc):
            lw = load_wchunk(w_h, oc, 128)

            def t_fm(t, oc=oc, oc0=oc0, lw=lw):
                t.wait_ge(lw[0].h, lw[1])
                mi = None
                for half in range(2):
                    ps = psA if half == 0 else psB
                    for k in range(D // 128):
                        mi = t.matmul(
                            ps[:, 0:512], wbuf[:, k * 128:(k + 1) * 128],
                            xlnT[:, k * TOK + half * 512:
                                 k * TOK + half * 512 + 512],
                            start=(k == 0), stop=(k == D // 128 - 1))
                return mi
            sec("tensor", t_fm)

            def v_fm(v, oc=oc, oc0=oc0):
                v.tensor_copy(fmbuf[:, (oc0 + oc) * TOK:
                                    (oc0 + oc) * TOK + 512], psA[:, 0:512])
                v.drain()
                return v.tensor_copy(fmbuf[:, (oc0 + oc) * TOK + 512:
                                           (oc0 + oc + 1) * TOK],
                                     psB[:, 0:512])
            sec("vector", v_fm)

    lw = load_wchunk(wkr_h, 0, RD)
    for m in range(TT):
        def t_kr(t, m=m, lw=lw):
            if m == 0:
                t.wait_ge(lw[0].h, lw[1])
            mi = None
            for k in range(D // 128):
                mi = t.matmul(
                    psA[:, 0:RD],
                    xlnT[:, k * TOK + m * 128:k * TOK + (m + 1) * 128],
                    wbuf[:, k * 128:k * 128 + RD],
                    start=(k == 0), stop=(k == D // 128 - 1))
            return mi
        sec("tensor", t_kr)

        def v_kr(v, m=m):
            v.tensor_copy(krt[:], psA[:, 0:RD])
            v.drain()
            v.tensor_tensor(krf[:, 0:RD], krt[:],
                            cosb[:, m * RD:(m + 1) * RD], ALU.mult)
            v.drain()
            v.tensor_tensor(krf[:, RD:RD + 32], krt[:, 32:64],
                            sgnb[:, m * RD:m * RD + 32], ALU.mult)
            v.drain()
            v.tensor_tensor(krf[:, RD + 32:2 * RD], krt[:, 0:32],
                            sgnb[:, m * RD + 32:(m + 1) * RD], ALU.mult)
            v.drain()
            return v.tensor_tensor(krr[:], krf[:, 0:RD], krf[:, RD:2 * RD],
                                   ALU.add)
        sec("vector", v_kr)

        emit_transpose(p, block, tiles=[(krr[:], krT[0:64, m * 128:(m + 1) * 128])],
                       ident=ident, pt=ptT, evac="vector",
                       wait_t=[last[0]], sems=tsem)
        last[0] = (tsem[1], tsem[1].n)

    @block.sync
    def _(sy):
        sy.wait_ge(last[0][0].h, last[0][1])
        for oc in range(16):
            sy.dma_start(out=out_o[ts(oc), :],
                         in_=fmbuf[:, oc * TOK:(oc + 1) * TOK]
                         ).then_inc(std.h, 16)
        sy.dma_start(out=out_o[2048:2112, :],
                     in_=krT[0:64, :]).then_inc(std.h, 16)
    std.add(16 * 17)

    @block.gpsimd
    def _(g):
        g.wait_ge(std.h, std.n)

    return p


def _rope_tables():
    inv = (1.0 / 10000.0 ** (np.arange(0, RD, 2, dtype=np.float32) / RD)
           ).astype(np.float32)
    t = np.arange(S, dtype=np.float32)
    fr = np.outer(t, inv).astype(np.float32)
    emb = np.concatenate([fr, fr], axis=1)
    cos = np.cos(emb).astype(np.float32)
    sin = np.sin(emb).astype(np.float32)
    sgn = np.concatenate([-sin[:, 0:32], sin[:, 32:64]], axis=1)
    return cos, sgn


def run_p4(inputs, x2_shards):
    wkv = np.ascontiguousarray(np.asarray(inputs["L1_attn_W_kv_down"], np.float32))
    wq = np.ascontiguousarray(np.asarray(inputs["L1_attn_W_q_down"], np.float32))
    wkr = np.ascontiguousarray(np.asarray(inputs["L1_attn_W_kr"], np.float32))
    cos, sgn = _rope_tables()
    p = build_p4()
    in_maps = []
    for c in range(NCORES):
        t0 = (c % 4) * TOK
        in_maps.append({"x": x2_shards[c], "wkv": wkv, "wq": wq, "wkr": wkr,
                        "cosw": np.ascontiguousarray(cos[t0:t0 + TOK]),
                        "sgnsin": np.ascontiguousarray(sgn[t0:t0 + TOK]),
                        "ident": _ident_bf16()})
    res = run_bass_kernel_spmd(p.nc, in_maps, core_ids=list(range(NCORES)))
    return res.results



NH = 4
SQ = S // 128


def build_p5():
    p = Prog("p5")
    cqg_h = p.din("cqg", [NH, QL + KVL + RD, TOK], BF16)
    wqc_h = p.din("wqc", [QL, NH * 64])
    wqr_h = p.din("wqr", [QL, NH * 64])
    wkc_h = p.din("wkc", [KVL, NH * 64])
    wv_h = p.din("wv", [KVL, NH * 128])
    cos_h = p.din("cosw", [S, RD])
    sgn_h = p.din("sgnsin", [S, RD])
    maskneg_h = p.din("maskneg", [128, 128])
    ident_h = p.din("ident", [128, 128], BF16)
    ctx_o = p.dout("ctx", [NH, S, 128], BF16)

    qFM = p.sbuf("qFM", [128, NH * S], BF16)
    kFM = p.sbuf("kFM", [128, NH * S], BF16)
    vTM = p.sbuf("vTM", [128, NH * S], BF16)
    cpan = p.sbuf("cpan", [128, 16 * 512], BF16)
    wqc = p.sbuf("wqcS", [128, (QL // 128) * NH * 64], BF16)
    wqr = p.sbuf("wqrS", [128, (QL // 128) * NH * 64], BF16)
    wkc = p.sbuf("wkcS", [128, (KVL // 128) * NH * 64], BF16)
    wv = p.sbuf("wvS", [128, (KVL // 128) * NH * 128], BF16)
    cosb = p.sbuf("cosb", [128, SQ * RD], F32)
    sgnb = p.sbuf("sgnb", [128, SQ * RD], F32)
    maskneg = p.sbuf("masknegS", [128, 128], F32)
    Pbuf = p.sbuf("Pbuf", [128, 2048], BF16)
    PT = p.sbuf("PTs", [128, 128], BF16)
    ctxa = p.sbuf("ctxa", [128, 128], F32)
    ctxo = p.sbuf("ctxo", [128, 128], BF16)
    qr_t = p.sbuf("qr_t", [128, RD], BF16)
    qr_r = p.sbuf("qr_r", [128, RD], BF16)
    qr_f = p.sbuf("qr_f", [128, 2 * RD], F32)
    stat = p.sbuf("stat", [128, 8], F32)
    ident = p.sbuf("identS", [128, 128], BF16)

    psS = p.psum("psS", [128, 2048], F32)
    psPV = p.psum("psPV", [128, 128], F32)
    psQ = p.psum("psQ", [128, 512], F32)
    ptT = (p.psum("ptTa", [128, 128], BF16), p.psum("ptTb", [128, 128], BF16))

    block = p.block()
    ld = p.sem()
    gld = p.sem()
    pld = p.sem()
    ust = p.sem()
    tsem = (p.sem(), p.sem())
    std = p.sem()
    last = [(ld, 0)]
    sec = make_sec(p, block, ust, last)
    SCL = float(1.0 / np.sqrt(128.0))

    @block.sync
    def _(sy):
        sy.dma_start(out=ident[:], in_=ident_h).then_inc(ld.h, 16)
        sy.dma_start(out=maskneg[:], in_=maskneg_h).then_inc(ld.h, 16)
        sy.dma_start(out=cosb[:].rearrange("p (m r) -> p m r", r=RD),
                     in_=cos_h.rearrange("(m p) r -> p m r", p=128)
                     ).then_inc(ld.h, 16)
        sy.dma_start(out=sgnb[:].rearrange("p (m r) -> p m r", r=RD),
                     in_=sgn_h.rearrange("(m p) r -> p m r", p=128)
                     ).then_inc(ld.h, 16)
    ld.add(64)

    @block.gpsimd
    def _(g):
        for k in range(QL // 128):
            g.dma_start(out=wqc[:, k * NH * 64:(k + 1) * NH * 64],
                        in_=wqc_h[ts(k), :]).then_inc(gld.h, 16)
            g.dma_start(out=wqr[:, k * NH * 64:(k + 1) * NH * 64],
                        in_=wqr_h[ts(k), :]).then_inc(gld.h, 16)
        for k in range(KVL // 128):
            g.dma_start(out=wkc[:, k * NH * 64:(k + 1) * NH * 64],
                        in_=wkc_h[ts(k), :]).then_inc(gld.h, 16)
            g.dma_start(out=wv[:, k * NH * 128:(k + 1) * NH * 128],
                        in_=wv_h[ts(k), :]).then_inc(gld.h, 16)
    gld.add(16 * (2 * (QL // 128) + 2 * (KVL // 128)))

    for tc in range(8):
        rank = tc // 2
        colh = (tc % 2) * 512

        @block.sync
        def _(sy, rank=rank, colh=colh):
            sy.wait_ge(last[0][0].h, last[0][1])
            for k in range(12):
                sy.dma_start(out=cpan[:, k * 512:(k + 1) * 512],
                             in_=cqg_h[rank, ts(k), colh:colh + 512]
                             ).then_inc(pld.h, 16)
            for k in range(4):
                sy.dma_start(out=cpan[:, (12 + k) * 512:(13 + k) * 512],
                             in_=cqg_h[rank, QL + k * 128:QL + (k + 1) * 128,
                                       colh:colh + 512]).then_inc(pld.h, 16)
            for hh in range(NH):
                sy.dma_start(
                    out=kFM[64:128, hh * S + tc * 512:hh * S + (tc + 1) * 512],
                    in_=cqg_h[rank, QL + KVL:QL + KVL + RD, colh:colh + 512]
                    ).then_inc(pld.h, 16)
        pld.add(16 * 20)
        lw = (pld, pld.n)

        for hh in range(NH):
            def t_qc(t, tc=tc, hh=hh, lw=lw):
                t.wait_ge(lw[0].h, lw[1])
                if tc == 0 and hh == 0:
                    t.wait_ge(gld.h, gld.n)
                mi = None
                for k in range(QL // 128):
                    mi = t.matmul(psQ[0:64, 0:512],
                                  wqc[:, k * NH * 64 + hh * 64:
                                      k * NH * 64 + (hh + 1) * 64],
                                  cpan[:, k * 512:(k + 1) * 512],
                                  start=(k == 0), stop=(k == QL // 128 - 1))
                return mi
            sec("tensor", t_qc)

            def v_qc(v, tc=tc, hh=hh):
                return v.tensor_copy(
                    qFM[0:64, hh * S + tc * 512:hh * S + (tc + 1) * 512],
                    psQ[0:64, 0:512])
            sec("vector", v_qc)

            def t_kc(t, tc=tc, hh=hh):
                mi = None
                for k in range(KVL // 128):
                    mi = t.matmul(psQ[0:64, 0:512],
                                  wkc[:, k * NH * 64 + hh * 64:
                                      k * NH * 64 + (hh + 1) * 64],
                                  cpan[:, (12 + k) * 512:(13 + k) * 512],
                                  start=(k == 0), stop=(k == KVL // 128 - 1))
                return mi
            sec("tensor", t_kc)

            def v_kc(v, tc=tc, hh=hh):
                return v.tensor_copy(
                    kFM[0:64, hh * S + tc * 512:hh * S + (tc + 1) * 512],
                    psQ[0:64, 0:512])
            sec("vector", v_kc)

            for tt in range(4):
                def t_v(t, tc=tc, hh=hh, tt=tt):
                    mi = None
                    for k in range(KVL // 128):
                        mi = t.matmul(
                            psQ[:, 0:128],
                            cpan[:, (12 + k) * 512 + tt * 128:
                                 (12 + k) * 512 + (tt + 1) * 128],
                            wv[:, k * NH * 128 + hh * 128:
                               k * NH * 128 + (hh + 1) * 128],
                            start=(k == 0), stop=(k == KVL // 128 - 1))
                    return mi
                sec("tensor", t_v)

                def v_v(v, tc=tc, hh=hh, tt=tt):
                    tok = tc * 4 + tt
                    return v.tensor_copy(
                        vTM[:, hh * S + tok * 128:hh * S + (tok + 1) * 128],
                        psQ[:, 0:128])
                sec("vector", v_v)

                def t_qr(t, tc=tc, hh=hh, tt=tt):
                    mi = None
                    for k in range(QL // 128):
                        mi = t.matmul(
                            psQ[:, 0:RD],
                            cpan[:, k * 512 + tt * 128:k * 512 + (tt + 1) * 128],
                            wqr[:, k * NH * 64 + hh * 64:
                                k * NH * 64 + (hh + 1) * 64],
                            start=(k == 0), stop=(k == QL // 128 - 1))
                    return mi
                sec("tensor", t_qr)

                def v_qr(v, tc=tc, hh=hh, tt=tt):
                    tok = tc * 4 + tt
                    v.tensor_copy(qr_t[:], psQ[:, 0:RD])
                    v.drain()
                    v.tensor_tensor(qr_f[:, 0:RD], qr_t[:],
                                    cosb[:, tok * RD:(tok + 1) * RD], ALU.mult)
                    v.drain()
                    v.tensor_tensor(qr_f[:, RD:RD + 32], qr_t[:, 32:64],
                                    sgnb[:, tok * RD:tok * RD + 32], ALU.mult)
                    v.drain()
                    v.tensor_tensor(qr_f[:, RD + 32:2 * RD], qr_t[:, 0:32],
                                    sgnb[:, tok * RD + 32:(tok + 1) * RD],
                                    ALU.mult)
                    v.drain()
                    return v.tensor_tensor(qr_r[:], qr_f[:, 0:RD],
                                           qr_f[:, RD:2 * RD], ALU.add)
                sec("vector", v_qr)

                tok = tc * 4 + tt
                emit_transpose(
                    p, block,
                    tiles=[(qr_r[:],
                            qFM[64:128, hh * S + tok * 128:
                                hh * S + (tok + 1) * 128])],
                    ident=ident, pt=ptT, evac="vector",
                    wait_t=[last[0]], sems=tsem)
                last[0] = (tsem[1], tsem[1].n)

    for hh in range(NH):
        for qt in range(SQ):
            klen = (qt + 1) * 128

            def v_init(v):
                v.memset(ctxa[:], 0.0)
                v.drain()
                v.memset(stat[:, 0:1], -3.0e38)
                v.drain()
                return v.memset(stat[:, 1:2], 0.0)
            sec("vector", v_init)

            nh_halves = (klen + 2047) // 2048
            for half in range(nh_halves):
                hk0 = half * 2048
                hklen = min(2048, klen - hk0)
                nch = (hklen + 511) // 512

                def t_sc(t, hh=hh, qt=qt, hk0=hk0, hklen=hklen, nch=nch):
                    mi = None
                    for ch in range(nch):
                        cw = min(512, hklen - ch * 512)
                        mi = t.matmul(
                            psS[:, ch * 512:ch * 512 + cw],
                            qFM[:, hh * S + qt * 128:hh * S + (qt + 1) * 128],
                            kFM[:, hh * S + hk0 + ch * 512:
                                hh * S + hk0 + ch * 512 + cw],
                            start=True, stop=True)
                    return mi
                sec("tensor", t_sc)

                if hk0 + hklen == klen:
                    def v_mask(v, hklen=hklen):
                        return v.tensor_tensor(
                            psS[:, hklen - 128:hklen],
                            psS[:, hklen - 128:hklen], maskneg[:], ALU.add)
                    sec("vector", v_mask)

                def v_mh(v, hklen=hklen):
                    v.tensor_reduce(stat[:, 2:3], psS[:, 0:hklen], AX.X,
                                    ALU.max)
                    v.drain()
                    v.tensor_tensor(stat[:, 3:4], stat[:, 0:1], stat[:, 2:3],
                                    ALU.max)
                    v.drain()
                    v.tensor_tensor(stat[:, 4:5], stat[:, 0:1], stat[:, 3:4],
                                    ALU.subtract)
                    v.drain()
                    v.tensor_scalar(stat[:, 4:5], stat[:, 4:5], SCL, None,
                                    ALU.mult)
                    v.drain()
                    v.tensor_scalar(stat[:, 5:6], stat[:, 3:4], -SCL, None,
                                    ALU.mult)
                    v.drain()
                    return v.tensor_copy(stat[:, 0:1], stat[:, 3:4])
                sec("vector", v_mh)

                def s_fa(s):
                    return s.activation(stat[:, 4:5], stat[:, 4:5], ACT.Exp)
                sec("scalar", s_fa)

                def s_exp(s, hklen=hklen):
                    return s.activation(Pbuf[:, 0:hklen], psS[:, 0:hklen],
                                        ACT.Exp, bias=stat[:, 5:6], scale=SCL,
                                        accum_out=stat[:, 6:7])
                sec("scalar", s_exp)

                def v_merge(v):
                    v.scalar_tensor_tensor(stat[:, 1:2], stat[:, 1:2],
                                           stat[:, 4:5], stat[:, 6:7],
                                           ALU.mult, ALU.add)
                    v.drain()
                    return v.tensor_scalar(ctxa[:], ctxa[:], stat[:, 4:5],
                                           None, ALU.mult)
                sec("vector", v_merge)

                nck = hklen // 128
                for ck in range(nck):
                    emit_transpose(
                        p, block,
                        tiles=[(Pbuf[:, ck * 128:(ck + 1) * 128], PT[:])],
                        ident=ident, pt=ptT, evac="vector",
                        wait_t=[last[0]], sems=tsem)
                    last[0] = (tsem[1], tsem[1].n)

                    def t_pv(t, hh=hh, hk0=hk0, ck=ck, nck=nck):
                        return t.matmul(
                            psPV[:],
                            PT[:],
                            vTM[:, hh * S + hk0 + ck * 128:
                                hh * S + hk0 + (ck + 1) * 128],
                            start=(ck == 0), stop=(ck == nck - 1))
                    sec("tensor", t_pv)

                def v_acc(v):
                    return v.tensor_tensor(ctxa[:], ctxa[:], psPV[:], ALU.add)
                sec("vector", v_acc)

            def v_fin(v):
                v.reciprocal(stat[:, 7:8], stat[:, 1:2])
                v.drain()
                return v.tensor_scalar(ctxo[:], ctxa[:], stat[:, 7:8], None,
                                       ALU.mult)
            sec("vector", v_fin)

            @block.sync
            def _(sy, hh=hh, qt=qt):
                sy.wait_ge(last[0][0].h, last[0][1])
                sy.dma_start(out=ctx_o[hh, ts(qt), :],
                             in_=ctxo[:]).then_inc(std.h, 16)
            std.add(16)
            def v_gate(v, n=std.n):
                v.wait_ge(std.h, n)
                return v.engine_nop()
            sec("vector", v_gate)

    @block.gpsimd
    def _(g):
        g.wait_ge(std.h, std.n)

    return p


def _maskneg():
    q = np.arange(128)[:, None]
    k = np.arange(128)[None, :]
    return np.where(k <= q, 0.0, -1.0e30).astype(np.float32)


def run_p5(inputs, p4_results):
    cq = [np.asarray(r["cqkvT"]) for r in p4_results]
    wqc = np.asarray(inputs["L1_attn_W_qc"], np.float32)
    wqr = np.asarray(inputs["L1_attn_W_qr"], np.float32)
    wkc = np.asarray(inputs["L1_attn_W_kc"], np.float32)
    wv = np.asarray(inputs["L1_attn_W_v"], np.float32)
    cos, sgn = _rope_tables()
    p = build_p5()
    in_maps = []
    for c in range(NCORES):
        b = c // 4
        heads = [(c % 4) * 4 + j for j in range(NH)]
        cqg = np.stack([cq[b * 4 + r] for r in range(4)], axis=0)
        in_maps.append({
            "cqg": cqg,
            "wqc": np.ascontiguousarray(
                np.concatenate([wqc[:, h * 64:(h + 1) * 64] for h in heads], 1)),
            "wqr": np.ascontiguousarray(
                np.concatenate([wqr[:, h * 64:(h + 1) * 64] for h in heads], 1)),
            "wkc": np.ascontiguousarray(
                np.concatenate([wkc[:, h * 64:(h + 1) * 64] for h in heads], 1)),
            "wv": np.ascontiguousarray(
                np.concatenate([wv[:, h * 128:(h + 1) * 128] for h in heads], 1)),
            "cosw": cos, "sgnsin": sgn,
            "maskneg": _maskneg(), "ident": _ident_bf16()})
    res = run_bass_kernel_spmd(p.nc, in_maps, core_ids=list(range(NCORES)))
    return res.results



TQ = 256
NTQ = TOK // TQ
FWC = 1.0 / float(np.sqrt(np.float32(I_FFN)))


def build_p6(dbg=False):
    p = Prog("p6")
    x_h = p.din("x", [TOK, D])
    ctxg_h = p.din("ctxg", [H, TOK, 128], BF16)
    wo_h = p.din("wo", [D, D])
    bo_h = p.din("bo", [1, D])
    gT_h = p.din("gTa", [D, I_FFN], BF16)
    uT_h = p.din("uTa", [D, I_FFN], BF16)
    dT_h = p.din("dTa", [I_FFN, D], BF16)
    had_h = p.din("had128", [128, 128], BF16)
    ident_h = p.din("ident", [128, 128], BF16)
    out_o = p.dout("xout", [TOK, D])
    if dbg:
        dbg_x3 = p.dout("dbg_x3", [TOK, D])
        dbg_xq = p.dout("dbg_xq", [D, TOK], BF16)
        dbg_h = p.dout("dbg_h", [256, I_FFN], BF16)
        dbg_bf = p.dout("dbg_bf", [256, I_FFN], BF16)
        dbg_fw = p.dout("dbg_fw", [256, I_FFN], BF16)
        dbg_q = p.dout("dbg_q", [256, I_FFN], BF16)
        dbgs = p.sem()

    x_sb = p.sbuf("x_sb", [128, TT * D], F32)
    xqT = p.sbuf("xqT", [128, (D // 128) * TOK], BF16)
    stat = p.sbuf("stat", [128, 8], F32)
    scr6 = p.sbuf("scr6", [128, 24], F32)
    xln = p.sbuf("xln", [128, D], F32)
    xq_t = p.sbuf("xq_t", [128, D], BF16)
    ident = p.sbuf("identS", [128, 128], BF16)
    had = p.sbuf("hadS", [128, 128], BF16)
    panels = Panels(p, "wpan", D // 128, 256)
    panD = Panels(p, "wpanD", 8, 512)

    psA = p.psum("psA", [128, 512], F32)
    psB = p.psum("psB", [128, 512], F32)
    ptT = (p.psum("ptTa", [128, 128], BF16), p.psum("ptTb", [128, 128], BF16))

    block = p.block()
    ld = p.sem()
    xload = p.sem()
    cld = p.sem()
    ust = p.sem()
    tsem = (p.sem(), p.sem())
    std = p.sem()
    dprog = p.sem()
    last = [(ld, 0)]
    sec = make_sec(p, block, ust, last)

    @block.sync
    def _(sy):
        sy.dma_start(out=ident[:], in_=ident_h).then_inc(ld.h, 16)
        sy.dma_start(out=had[:], in_=had_h).then_inc(ld.h, 16)
        for m in range(TT):
            sy.dma_start(out=x_sb[:, m * D:(m + 1) * D],
                         in_=x_h[ts(m), :]).then_inc(xload.h, 16)
    ld.add(32)
    xload.add(16 * TT)

    es2 = contextlib.ExitStack()
    ctxT = es2.enter_context(p.nc.sbuf_tensor("ctxT", [128, H * TOK], BF16))
    ctmp = es2.enter_context(p.nc.sbuf_tensor("ctmp", [128, 128], BF16))
    bob = es2.enter_context(p.nc.sbuf_tensor("bob", [128, D], F32))

    bld = p.sem()

    @block.gpsimd
    def _(g):
        g.dma_start(out=bob[:],
                    in_=bo_h.broadcast_to([128, D])).then_inc(bld.h, 16)
    bld.add(16)

    for hhead in range(H):
        for m in range(TT):
            @block.sync
            def _(sy, hhead=hhead, m=m):
                sy.wait_ge(last[0][0].h, last[0][1])
                sy.dma_start(out=ctmp[:],
                             in_=ctxg_h[hhead, ts(m), :]).then_inc(cld.h, 16)
            cld.add(16)

            emit_transpose(
                p, block,
                tiles=[(ctmp[:],
                        ctxT[:, hhead * TOK + m * 128:
                             hhead * TOK + (m + 1) * 128])],
                ident=ident, pt=ptT, evac="vector",
                wait_t=[last[0], (cld, cld.n), (ld, 16)], sems=tsem)
            last[0] = (tsem[1], tsem[1].n)

    def wo_evac(e, m, n, ps, idx):
        e.tensor_tensor(x_sb[:, m * D + n * 256:m * D + (n + 1) * 256], ps,
                        x_sb[:, m * D + n * 256:m * D + (n + 1) * 256],
                        ALU.add)
        e.drain()
        return e.tensor_tensor(
            x_sb[:, m * D + n * 256:m * D + (n + 1) * 256],
            x_sb[:, m * D + n * 256:m * D + (n + 1) * 256],
            bob[:, n * 256:(n + 1) * 256], ALU.add)

    wo_ev = gemm_tm(p, block,
                    lhsT_tile=lambda k, m: ctxT[:, k * TOK + m * 128:
                                                k * TOK + (m + 1) * 128],
                    k_chunks=H, n_chunks=D // 256, nb=256,
                    panels=panels,
                    panel_srcs=lambda n: [wo_h[ts(k), n * 256:(n + 1) * 256]
                                          for k in range(D // 128)],
                    cast=True, evac_engine="vector", evac_emit=wo_evac,
                    wait_t=[last[0], (xload, xload.n), (bld, bld.n)],
                    psA=psA, psB=psB)
    last[0] = (wo_ev, wo_ev.n)
    es2.close()

    for m in range(TT):
        emit_ln(p, block, x_ap=x_sb[:, m * D:(m + 1) * D], out_ap=xln[:],
                stat=stat, scr6=scr6, wait_v=[last[0]], done=ust)
        last[0] = (ust, ust.n)

        def v_aq(v, m=m):
            v.tensor_reduce(stat[:, 0:1], xln[:], AX.X, ALU.max,
                            apply_absolute_value=True)
            v.drain()
            v.tensor_scalar(stat[:, 1:2], stat[:, 0:1], 1e-5, None, ALU.max)
            v.drain()
            v.reciprocal(stat[:, 2:3], stat[:, 1:2])
            v.drain()
            v.tensor_scalar(stat[:, 2:3], stat[:, 2:3], 128.0, None, ALU.mult)
            v.drain()
            v.tensor_scalar(stat[:, 3:4], stat[:, 1:2], 1.0 / 128.0, None,
                            ALU.mult)
            v.drain()
            v.tensor_scalar(xln[:], xln[:], stat[:, 2:3], MAGIC,
                            ALU.mult, ALU.add)
            v.drain()
            v.tensor_scalar(xln[:], xln[:], MAGIC, 127.0,
                            ALU.subtract, ALU.min)
            v.drain()
            return v.tensor_scalar(xq_t[:], xln[:], stat[:, 3:4], None,
                                   ALU.mult)
        sec("vector", v_aq)

        emit_transpose(
            p, block,
            tiles=[(xq_t[:, c * 128:(c + 1) * 128],
                    xqT[:, c * TOK + m * 128:c * TOK + (m + 1) * 128])
                   for c in range(D // 128)],
            ident=ident, pt=ptT, evac="vector",
            wait_t=[last[0]], sems=tsem)
        last[0] = (tsem[1], tsem[1].n)

    if dbg:
        @block.sync
        def _(sy):
            sy.wait_ge(last[0][0].h, last[0][1])
            for m in range(TT):
                sy.dma_start(out=dbg_x3[ts(m), :],
                             in_=x_sb[:, m * D:(m + 1) * D]
                             ).then_inc(dbgs.h, 16)
            for c in range(D // 128):
                sy.dma_start(out=dbg_xq[ts(c), :],
                             in_=xqT[:, c * TOK:(c + 1) * TOK]
                             ).then_inc(dbgs.h, 16)
        dbgs.add(16 * (TT + D // 128))
    es3 = contextlib.ExitStack()
    hb = es3.enter_context(p.nc.sbuf_tensor("hb", [128, 2 * I_FFN], BF16))
    h2 = es3.enter_context(p.nc.sbuf_tensor("h2", [128, 2 * I_FFN], BF16))

    for tq in range(NTQ):
        m0 = tq * 2

        def silu_evac(e, m, n, ps, idx):
            return e.activation(hb[:, m * I_FFN + n * 256:
                                   m * I_FFN + (n + 1) * 256], ps, ACT.Silu)

        g_ev = gemm_tm(p, block,
                       lhsT_tile=lambda k, m, m0=m0: xqT[
                           :, k * TOK + (m0 + m) * 128:
                           k * TOK + (m0 + m + 1) * 128],
                       k_chunks=D // 128, n_chunks=I_FFN // 256, nb=256,
                       panels=panels, m_tiles=2,
                       panel_srcs=lambda n: [gT_h[ts(k), n * 256:(n + 1) * 256]
                                             for k in range(D // 128)],
                       cast=True, evac_engine="scalar", evac_emit=silu_evac,
                       wait_t=[last[0]], psA=psA, psB=psB)

        def up_evac(e, m, n, ps, idx):
            return e.tensor_tensor(
                hb[:, m * I_FFN + n * 256:m * I_FFN + (n + 1) * 256], ps,
                hb[:, m * I_FFN + n * 256:m * I_FFN + (n + 1) * 256],
                ALU.mult)

        u_ev = gemm_tm(p, block,
                       lhsT_tile=lambda k, m, m0=m0: xqT[
                           :, k * TOK + (m0 + m) * 128:
                           k * TOK + (m0 + m + 1) * 128],
                       k_chunks=D // 128, n_chunks=I_FFN // 256, nb=256,
                       panels=panels, m_tiles=2,
                       panel_srcs=lambda n: [uT_h[ts(k), n * 256:(n + 1) * 256]
                                             for k in range(D // 128)],
                       cast=True, evac_engine="vector", evac_emit=up_evac,
                       wait_t=[(g_ev, g_ev.n)], psA=psA, psB=psB)
        last[0] = (u_ev, u_ev.n)

        if dbg and tq == 0:
            @block.sync
            def _(sy):
                sy.wait_ge(last[0][0].h, last[0][1])
                for tt2 in range(2):
                    sy.dma_start(out=dbg_h[ts(tt2), :],
                                 in_=hb[:, tt2 * I_FFN:(tt2 + 1) * I_FFN]
                                 ).then_inc(dbgs.h, 16)
            dbgs.add(32)
            def v_np0(v):
                v.wait_ge(dbgs.h, dbgs.n)
                return v.engine_nop()
            sec("vector", v_np0)

        for tt in range(2):
            for s in range(6):
                src_b = hb if s % 2 == 0 else h2
                dst_b = h2 if s % 2 == 0 else hb
                step = 128 * (2 ** s)
                ngrp = I_FFN // (2 * step)

                def v_bf(v, tt=tt, src_b=src_b, dst_b=dst_b, step=step,
                         ngrp=ngrp):
                    base = tt * I_FFN
                    a_lo = src_b[:, base:base + I_FFN].rearrange(
                        "p (g two i) -> p g two i", two=2, i=step)[:, :, 0, :]
                    a_hi = src_b[:, base:base + I_FFN].rearrange(
                        "p (g two i) -> p g two i", two=2, i=step)[:, :, 1, :]
                    o_lo = dst_b[:, base:base + I_FFN].rearrange(
                        "p (g two i) -> p g two i", two=2, i=step)[:, :, 0, :]
                    o_hi = dst_b[:, base:base + I_FFN].rearrange(
                        "p (g two i) -> p g two i", two=2, i=step)[:, :, 1, :]
                    v.tensor_tensor(o_lo, a_lo, a_hi, ALU.add)
                    v.drain()
                    return v.tensor_tensor(o_hi, a_lo, a_hi, ALU.subtract)
                sec("vector", v_bf)

        if dbg and tq == 0:
            @block.sync
            def _(sy):
                sy.wait_ge(last[0][0].h, last[0][1])
                for tt2 in range(2):
                    sy.dma_start(out=dbg_bf[ts(tt2), :],
                                 in_=hb[:, tt2 * I_FFN:(tt2 + 1) * I_FFN]
                                 ).then_inc(dbgs.h, 16)
            dbgs.add(32)
            def v_np1(v):
                v.wait_ge(dbgs.h, dbgs.n)
                return v.engine_nop()
            sec("vector", v_np1)

        for tt in range(2):
            emit_transpose(
                p, block,
                tiles=[(hb[:, tt * I_FFN + fc * 128:tt * I_FFN + (fc + 1) * 128],
                        h2[:, fc * TQ + tt * 128:fc * TQ + (tt + 1) * 128])
                       for fc in range(64)],
                ident=ident, pt=ptT, evac="vector",
                wait_t=[last[0]], sems=tsem)
            last[0] = (tsem[1], tsem[1].n)

        for fc in range(64):
            for tt in range(2):
                def t_h128(t, fc=fc, tt=tt):
                    return t.matmul(psA[:, 0:128],
                                    h2[:, fc * TQ + tt * 128:
                                       fc * TQ + (tt + 1) * 128],
                                    had[:], start=True, stop=True)
                sec("tensor", t_h128)

                def s_ev(s, fc=fc, tt=tt):
                    return s.activation(
                        hb[:, tt * I_FFN + fc * 128:tt * I_FFN + (fc + 1) * 128],
                        psA[:, 0:128], ACT.Copy, scale=FWC)
                sec("scalar", s_ev)

        if dbg and tq == 0:
            @block.sync
            def _(sy):
                sy.wait_ge(last[0][0].h, last[0][1])
                for tt2 in range(2):
                    sy.dma_start(out=dbg_fw[ts(tt2), :],
                                 in_=hb[:, tt2 * I_FFN:(tt2 + 1) * I_FFN]
                                 ).then_inc(dbgs.h, 16)
            dbgs.add(32)
            def v_np2(v):
                v.wait_ge(dbgs.h, dbgs.n)
                return v.engine_nop()
            sec("vector", v_np2)

        for tt in range(2):
            def v_aq2(v, tt=tt):
                v.tensor_reduce(stat[:, 0:1], hb[:, tt * I_FFN:(tt + 1) * I_FFN],
                                AX.X, ALU.max, apply_absolute_value=True)
                v.drain()
                v.tensor_scalar(stat[:, 1:2], stat[:, 0:1], 1e-5, None,
                                ALU.max)
                v.drain()
                v.reciprocal(stat[:, 2:3], stat[:, 1:2])
                v.drain()
                v.tensor_scalar(stat[:, 2:3], stat[:, 2:3], 128.0, None,
                                ALU.mult)
                v.drain()
                v.tensor_scalar(stat[:, 3:4], stat[:, 1:2], 1.0 / 128.0, None,
                                ALU.mult)
                v.drain()
                ins = None
                for c4 in range(I_FFN // D):
                    sl = slice(tt * I_FFN + c4 * D, tt * I_FFN + (c4 + 1) * D)
                    v.tensor_scalar(xln[:], hb[:, sl],
                                    stat[:, 2:3], MAGIC, ALU.mult, ALU.add)
                    v.drain()
                    v.tensor_scalar(xln[:], xln[:],
                                    MAGIC, 127.0, ALU.subtract, ALU.min)
                    v.drain()
                    ins = v.tensor_scalar(h2[:, sl], xln[:],
                                          stat[:, 3:4], None, ALU.mult)
                    v.drain()
                return ins
            sec("vector", v_aq2)

        if dbg and tq == 0:
            @block.sync
            def _(sy):
                sy.wait_ge(last[0][0].h, last[0][1])
                for tt2 in range(2):
                    sy.dma_start(out=dbg_q[ts(tt2), :],
                                 in_=h2[:, tt2 * I_FFN:(tt2 + 1) * I_FFN]
                                 ).then_inc(dbgs.h, 16)
            dbgs.add(32)
            def v_np3(v):
                v.wait_ge(dbgs.h, dbgs.n)
                return v.engine_nop()
            sec("vector", v_np3)

        for tt in range(2):
            emit_transpose(
                p, block,
                tiles=[(h2[:, tt * I_FFN + fc * 128:tt * I_FFN + (fc + 1) * 128],
                        hb[:, fc * TQ + tt * 128:fc * TQ + (tt + 1) * 128])
                       for fc in range(64)],
                ident=ident, pt=ptT, evac="vector",
                wait_t=[last[0]], sems=tsem)
            last[0] = (tsem[1], tsem[1].n)

        for n in range(D // 512):
            gws = []
            for kg in range(8):
                rel = ((ust, ust.n + 1) if kg == 7
                       else (dprog, dprog.n + kg + 1))
                gws.append(panD.load_gen(
                    block,
                    [dT_h[kg * 1024 + k * 128:kg * 1024 + (k + 1) * 128,
                          n * 512:(n + 1) * 512] for k in range(8)],
                    True, rel))

            def t_dn(t, n=n, gws=gws, tq=tq):
                t.wait_ge(last[0][0].h, last[0][1])
                mi = None
                for kg in range(8):
                    t.wait_ge(panD.loaded.h, gws[kg])
                    for mloc in range(2):
                        ps = psA if mloc == 0 else psB
                        for k in range(8):
                            mi = t.matmul(
                                ps[:, 0:512],
                                hb[:, (kg * 8 + k) * TQ + mloc * 128:
                                   (kg * 8 + k) * TQ + (mloc + 1) * 128],
                                panD.buf(panD.gen - 8 + kg, k),
                                start=(kg == 0 and k == 0),
                                stop=(kg == 7 and k == 7))
                    if kg < 7:
                        mi.then_inc(dprog.h, 1)
                return mi
            sec("tensor", t_dn)
            dprog.add(7)

            def v_dn(v, n=n, tq=tq):
                for mloc in range(2):
                    m = tq * 2 + mloc
                    ps = psA if mloc == 0 else psB
                    ins = v.tensor_tensor(
                        x_sb[:, m * D + n * 512:m * D + (n + 1) * 512],
                        ps[:, 0:512],
                        x_sb[:, m * D + n * 512:m * D + (n + 1) * 512],
                        ALU.add)
                    if mloc == 0:
                        v.drain()
                return ins
            sec("vector", v_dn)

    es3.close()

    @block.sync
    def _(sy):
        sy.wait_ge(last[0][0].h, last[0][1])
        for m in range(TT):
            sy.dma_start(out=out_o[ts(m), :],
                         in_=x_sb[:, m * D:(m + 1) * D]).then_inc(std.h, 16)
    std.add(16 * TT)

    @block.gpsimd
    def _(g):
        g.wait_ge(std.h, std.n)

    return p


def _had128():
    import ml_dtypes
    i = np.arange(128)
    popc = np.zeros((128, 128), np.int64)
    for b in range(7):
        popc += ((i[:, None] >> b) & 1) * ((i[None, :] >> b) & 1)
    return ((-1.0) ** (popc % 2)).astype(ml_dtypes.bfloat16)


def run_p6(inputs, x2_shards, p5_results, w2_results):
    wo = np.ascontiguousarray(np.asarray(inputs["L1_attn_W_o"], np.float32))
    bo = np.ascontiguousarray(
        np.asarray(inputs["L1_attn_b_o"], np.float32).reshape(1, D))
    gTa = np.concatenate([np.asarray(r["gT"]) for r in w2_results], axis=1)
    uTa = np.concatenate([np.asarray(r["uT"]) for r in w2_results], axis=1)
    dTa = np.concatenate([np.asarray(r["dT"]) for r in w2_results], axis=1)
    ctx = [np.asarray(r["ctx"]) for r in p5_results]
    p = build_p6()
    in_maps = []
    for c in range(NCORES):
        b = c // 4
        t0 = (c % 4) * TOK
        ctxg = np.stack([ctx[b * 4 + (h // 4)][h % 4, t0:t0 + TOK, :]
                         for h in range(H)], axis=0)
        in_maps.append({"x": x2_shards[c], "ctxg": np.ascontiguousarray(ctxg),
                        "wo": wo, "bo": bo, "gTa": gTa, "uTa": uTa,
                        "dTa": dTa, "had128": _had128(),
                        "ident": _ident_bf16()})
    res = run_bass_kernel_spmd(p.nc, in_maps, core_ids=list(range(NCORES)))
    return res.results



def kernel(**inputs):
    w1_results = run_w1(inputs)
    w2_results = run_w2(inputs, w1_results)
    p1_results = run_p1(inputs)
    p2_results = run_p2(inputs, p1_results)
    x1_shards = [np.ascontiguousarray(np.asarray(r["x1"])) for r in p2_results]
    p3_results = run_p3(inputs, x1_shards, w1_results)
    x2_shards = [np.ascontiguousarray(np.asarray(r["x2"])) for r in p3_results]
    p4_results = run_p4(inputs, x2_shards)
    p5_results = run_p5(inputs, p4_results)
    p6_results = run_p6(inputs, x2_shards, p5_results, w2_results)
    out = np.concatenate([np.asarray(r["xout"]) for r in p6_results], axis=0)
    return np.ascontiguousarray(out.reshape(B, S, D).astype(np.float32))
